# version 05
# speedup vs baseline: 1.7285x; 1.0334x over previous
import math

import numpy as np
import ml_dtypes

import concourse.bass as bass
import concourse.mybir as mybir
from concourse.bass_utils import run_bass_kernel_spmd

AP = bass.AP
F32 = mybir.dt.float32
BF16 = mybir.dt.bfloat16
AL = mybir.AluOpType
ACT = mybir.ActivationFunctionType
AX = mybir.AxisListType.X
BF = ml_dtypes.bfloat16

B, L, D = 4, 8192, 1024
TLOC = L // 2
NH, DH, NLM, SEG = 16, 64, 128, 64
IFF = 2816
QSC = 1.0 / math.sqrt(math.sqrt(DH))
EPS = 1e-5
WSH_GU = IFF // 8
WSH_D = D // 8
RUNGS_U = [9.0, 25.0, 49.0]
RUNGS_W = [(100.0, 2.0), (196.0, 2.0), (400.0, 4.0)]
KB = 1024


class Sched:

    ENGINES = ("sync", "scalar", "vector", "tensor", "gpsimd")

    def __init__(self, nc):
        self.nc = nc
        self.items = []
        self.ctr = {}

    def sem(self, name):
        if name not in self.ctr:
            self.ctr[name] = [self.nc.alloc_semaphore(name), 0]
        return self.ctr[name]

    def val(self, name):
        return self.sem(name)[1]

    def add(self, eng, emit, waits=(), inc=None, inck=1):
        w = [(self.sem(n)[0], v) for (n, v) in waits]
        incs = []
        if inc is not None:
            se = self.sem(inc)
            se[1] += inck
            incs = [(se[0], inck)]
        self.items.append((eng, emit, w, incs))
        return self.val(inc) if inc else None

    def dma(self, eng, dst, src, inc, waits=(), **kw):
        return self.add(eng, lambda e: e.dma_start(dst, src, **kw),
                        waits=waits, inc=inc, inck=16)

    def dmaT(self, eng, dst, src, inc, waits=()):
        return self.add(eng, lambda e: e.dma_start(dst, src, transpose=True),
                        waits=waits, inc=inc, inck=16)

    def drain(self, eng):
        self.add(eng, lambda e: e.drain())

    def emit(self):
        items, self.items = self.items, []
        with self.nc.Block() as blk:
            for eng_name in self.ENGINES:
                ops = [it for it in items if it[0] == eng_name]
                if not ops:
                    continue

                def section(eng, ops=ops):
                    for (_e, emit_fn, waits, incs) in ops:
                        for (sem, v) in waits:
                            eng.wait_ge(sem, v)
                        r = emit_fn(eng)
                        if incs:
                            last = r[-1] if isinstance(r, (list, tuple)) else r
                            for (sem, k) in incs:
                                last.then_inc(sem, k)

                getattr(blk, eng_name)(section)


def ap3(t, off, dims):
    a = t.ap()
    return AP(a.tensor, off, dims)


def dr_rows(h, r0, nr, nc_):
    return AP(h, r0 * nc_, [[nc_, nr], [1, nc_]])


def dr_blk(h, r0, nr, c0, ncs, nc_):
    return AP(h, r0 * nc_ + c0, [[nc_, nr], [1, ncs]])


class Prog:
    def __init__(self, dbg=(), phases=99, projs=("K", "V", "Q"), sim1=False):
        self.dbg = tuple(dbg)
        self.phases = phases
        self.projs = tuple(projs)
        self.sim1 = sim1
        nc = bass.Bass("TRN2", target_bir_lowering=False)
        self.nc = nc
        self.s = Sched(nc)
        self._io()
        self._sbuf()
        self.build()

    def _io(self):
        nc = self.nc
        self.i = {}
        for name, shp in [
            ("x", [L, D]), ("wqT", [D, D]), ("wkT", [D, D]), ("wvT", [D, D]),
            ("bq", [1, D]), ("bk", [1, D]), ("bv", [1, D]),
            ("wg_sh", [WSH_GU, D]), ("wu_sh", [WSH_GU, D]), ("wd_sh", [D, WSH_GU]),
            ("wuT", [D, D]), ("wgT", [D, D]), ("bu", [1, D]), ("bg", [1, D]),
            ("Am", [D, D]), ("bmT", [D, D]), ("cmT", [D, D]), ("Dv", [1, D]),
            ("win", [D, 64]), ("binf", [1, 64]), ("w1", [64, 16]), ("b1", [64, 16]),
            ("w2", [64, 16]), ("b2", [1, 64]), ("wout", [64, D]), ("bout", [1, D]),
            ("cmask", [1, 8]),
        ]:
            self.i[name] = nc.dram_tensor(name, shp, F32, kind="ExternalInput")
        self.o_out = nc.dram_tensor("out", [TLOC, D], F32, kind="ExternalOutput")
        self.o_dbg = {}
        for d in self.dbg:
            shp, dt = {
                "x1": ([TLOC, D], F32), "x2": ([TLOC, D], F32),
                "x3": ([TLOC, D], F32), "attn": ([TLOC, D], F32),
                "qlnx": ([TLOC, D], BF16), "qh": ([TLOC, IFF], BF16),
                "k2dbg": ([128, NH * 128], BF16), "invT": ([128, NH * 128], BF16),
                "xlm": ([128, 8 * 128], BF16), "xs": ([L, D], BF16),
                "kk": ([D, L], BF16), "qq": ([D, TLOC], BF16),
                "uB": ([128, 8 * TLOC], BF16), "S0": ([128, 8 * TLOC], BF16),
                "stats": ([4, L], F32), "wgq": ([8 * D, WSH_GU], BF16),
                "wdq": ([IFF, D], BF16),
                "xsb": ([128, 8 * 512], BF16),
                "p3t": ([128, 2 * L], BF16), "pvn": ([128, 2 * 64], BF16),
                "m2d": ([128, 2 * 65], BF16), "p1d1": ([128, 2 * TLOC], BF16),
                "attd": ([128, 32 * 130], F32), "vtd": ([128, 64 * 160], BF16),
            }[d]
            self.o_dbg[d] = nc.dram_tensor("dbg_" + d, shp, dt, kind="ExternalOutput")

        h = {}
        for name, shp, dt in [
            ("xs", [L, D], BF16), ("stats", [4, L], F32),
            ("K", [D, L], BF16), ("V", [D, L], BF16), ("Q", [D, TLOC], BF16),
            ("attn", [TLOC, D], F32), ("res", [TLOC, D], F32),
            ("qlnx", [TLOC, D], BF16), ("qh", [TLOC, IFF], BF16),
            ("xs3", [TLOC, D], BF16), ("xs4", [TLOC, D], BF16),
            ("wuT16", [D, D], BF16), ("wgT16", [D, D], BF16),
            ("bmT16", [D, D], BF16), ("cmT16", [D, D], BF16),
            ("row", [8, L], F32),
        ]:
            h[name] = nc.dram_tensor("h_" + name, shp, dt)
        self.h = h
        cc = {}
        for name, shp, dt in [
            ("gmax_in", [1, 8], F32), ("gmax_out", [1, 8], F32),
            ("wg_in", [D, WSH_GU], BF16), ("wg_out", [8 * D, WSH_GU], BF16),
            ("wu_in", [D, WSH_GU], BF16), ("wu_out", [8 * D, WSH_GU], BF16),
            ("wd_in", [WSH_GU, D], BF16), ("wd_out", [IFF, D], BF16),
            ("carry_in", [1, D], F32), ("carry_out", [2, D], F32),
        ]:
            cc[name] = nc.dram_tensor("c_" + name, shp, dt)
        self.c = cc
        self.t_eye_h = nc.inline_tensor(np.eye(128, dtype=np.float32), "eye128")

    def _sbuf(self):
        nc = self.nc
        self.t_pbig = nc.alloc_psum_tensor("pbig", [128, 4096], F32)
        self.pb = [self.t_pbig[0:128, i * 512:(i + 1) * 512] for i in range(8)]


        BASE = 17408

        def at(name, shape, dtype, off):
            off += BASE
            assert off + shape[1] * mybir.dt.size(dtype) <= 229376, name
            t = nc.alloc_sbuf_tensor_at(name, shape, dtype, offset=off)
            setattr(self, "t_" + name, t)
            return t

        at("stats", [128, 256], F32, 0)
        at("eye", [128, 128], F32, 1024)
        at("eye7", [128, 128], F32, 1536)
        at("eye15", [128, 128], F32, 2048)
        at("eye325", [128, 128], F32, 2560)
        at("Xlm", [128, 8 * 128], BF16, 3072)
        at("Qlm", [128, 8 * 128], BF16, 5120)
        at("Klm", [128, 8 * 128], BF16, 7168)
        at("k2", [128, NH * 128], BF16, 9216)
        at("ViT", [128, NH * 128], BF16, 21504)
        at("eyeb2", [128, 128], BF16, 25600)
        at("eyeb4", [128, 128], BF16, 25856)
        at("small", [128, 64], F32, 37888)
        at("onesb", [128, 1], BF16, 38784)
        at("gmax", [1, 32], F32, 38144)
        at("pmur", [2, 128], F32, 38272)
        at("cmov", [2, 512], F32, 38912)
        at("onesr", [1, 1024], F32, 40960)
        for j, p in enumerate(["q", "k", "v", "u", "g", "in"]):
            at("cs_" + p, [2, D], F32, 45056 + j * 4096)
        at("win16", [128, 8 * 64], BF16, 69632)
        at("wout16", [64, D], BF16, 70656)
        at("w1s", [64, 16], F32, 72704)
        at("b1s", [64, 16], F32, 72768)
        at("w2s", [64, 16], F32, 72832)
        at("b2s", [64, 1], F32, 72896)
        at("eyeb", [128, 128], BF16, 72960)
        at("Drow", [1, D], F32, 73728)
        at("borow", [1, D], F32, 77824)
        PZ = 80 * KB
        self.PZ = PZ
        at("xa", [128, 1024], F32, PZ)
        at("xb", [128, 1024], F32, PZ + 4 * KB)
        at("junk", [128, 1024], F32, PZ + 8 * KB)
        at("cola", [128, 64], F32, PZ + 12 * KB)
        at("colb", [128, 64], F32, PZ + 12544)
        at("colc", [128, 64], F32, PZ + 13312)
        at("cold", [128, 64], F32, PZ + 13824)
        at("xso", [128, 1024], BF16, PZ + 14 * KB)
        at("vi32", [128, 2048], F32, PZ + 16 * KB)
        at("vit32", [128, 2048], F32, PZ + 24 * KB)
        at("gel32", [128, 2048], F32, PZ + 32 * KB)
        at("gelt32", [128, 2048], F32, PZ + 40 * KB)
        at("kvt32", [128, 2048], F32, PZ + 48 * KB)
        at("k2t32", [128, 2048], F32, PZ + 56 * KB)
        at("e2s", [128, 128], F32, PZ + 66 * KB)
        at("dgs", [128, 128], F32, PZ + 66 * KB + 512)
        at("k2f32", [128, 2048], F32, PZ + 118 * KB)
        at("q1", [128, 2816], F32, PZ + 16 * KB)
        at("q2", [128, 2816], F32, PZ + 27 * KB)
        at("q3", [128, 2816], F32, PZ + 38 * KB)
        at("p0sg", [128, 2048], BF16, PZ + 38 * KB)
        at("q4", [128, 2816], F32, PZ + 49 * KB)
        at("qo", [128, 2816], BF16, PZ + 60 * KB)
        at("qm", [128, 3 * 176], F32, PZ + 66 * KB)
        at("w3", [128, 3 * 8 * 1024], BF16, PZ + 70 * KB)
        at("xsblk", [128, 8 * 512], BF16, PZ + 118 * KB)
        at("stg", [128, 8 * 512], BF16, PZ + 16 * KB)
        at("stg2", [128, 8 * 512], BF16, PZ + 24 * KB)
        at("K2", [128, L], BF16, PZ)
        at("atn", [128, 32 * 128], F32, PZ + 16 * KB)
        at("Vt", [128, 64 * 160], BF16, PZ + 32 * KB)
        at("P3T", [128, 2 * L], BF16, PZ + 52 * KB)
        at("Q2", [128, TLOC], BF16, PZ + 84 * KB)
        at("P1", [128, 2 * TLOC], BF16, PZ + 92 * KB)
        at("att", [128, 32 * 130], F32, PZ + 108 * KB)
        at("rden", [128, 64], F32, PZ + 125 * KB)
        at("PVn", [128, 2 * 64], BF16, PZ + 128512)
        at("M2", [128, 2 * 65], BF16, PZ + 126 * KB)
        at("f_ln4", [128, 4096], F32, PZ + 53248)
        at("f_m", [128, 512], F32, PZ + 69632)
        at("f_qo", [128, 4096], BF16, PZ + 71680)
        at("f_u", [128, 4096], F32, PZ + 79872)
        at("f_sgb", [128, 8192], BF16, PZ + 96256)
        at("qlxf", [128, 8 * 512], BF16, PZ)
        at("h4", [128, 4 * IFF], BF16, PZ + 8192)
        at("wsl", [128, 2 * 16 * WSH_GU], BF16, PZ + 30720)
        at("qhf", [128, 22 * 512], BF16, PZ + 53248)
        at("wd2", [128, 2 * 1024], BF16, PZ + 75776)
        at("sgs", [128, 1008], F32, PZ + 79872)
        at("p2m", [128, 2 * 704], F32, PZ + 83904)
        at("p2o", [128, 2816], BF16, PZ + 89536)
        at("p2u", [128, 2816], F32, PZ + 95168)
        at("p2sg", [128, 2 * 2816], BF16, PZ + 106432)
        at("x4t", [128, 2 * 1024], F32, PZ + 117696)
        at("cmsl", [128, 2 * 1024], BF16, PZ + 125888)
        at("uB", [128, 8 * TLOC], BF16, PZ)
        at("x3f", [128, 8 * 512], BF16, PZ + 64 * KB)
        at("gtd", [128, 8 * 512], BF16, PZ + 72 * KB)
        at("ugu", [128, 8 * 512], BF16, PZ + 80 * KB)
        at("ugs", [128, 512], F32, PZ + 88 * KB)
        at("wst", [128, 2 * 1024], BF16, PZ + 90 * KB)
        at("ctmp", [128, TLOC], BF16, PZ + 94 * KB)
        at("ccar", [128, 16], F32, 73216)
        at("xso3", [128, 2 * 1024], BF16, PZ + 102 * KB)
        at("s4", [64, TLOC], F32, PZ + 16 * KB)
        at("hia", [64, TLOC], F32, PZ + 32 * KB)
        at("hib", [64, TLOC], F32, PZ + 48 * KB)
        at("g4", [64, TLOC], F32, PZ + 64 * KB)
        at("g4b", [64, TLOC], BF16, PZ + 80 * KB)
        at("x4f", [128, 8 * 512], BF16, PZ + 88 * KB)

    def col(self, t, c):
        return t[:, c:c + 1]

    def bc_col(self, t, c, n, rows=128):
        a = t.ap()
        return AP(a.tensor, c, [[a.ap[0][0], rows], [0, n]])

    def build(self):
        self.p0_consts()
        if self.phases >= 1:
            self.p0_weights()
        if self.phases >= 2:
            self.p1_stats_xs()
        if self.phases >= 3:
            self.p1_proj()
        if self.phases >= 4:
            self.p1_landmarks()
        if self.phases >= 5:
            self.p1_pairs()
        if self.phases >= 6:
            self.p1_x1()
        if self.phases >= 7:
            self.p2_ffn()
        if self.phases >= 8:
            self.p3_ssm()
        if self.phases >= 9:
            self.p4_mlp()
        self.finish()

    def p0_consts(self):
        s = self.s
        ey0 = s.dma("sync", self.t_eye[:, :], self.t_eye_h.ap(), inc="SY")
        emits = [
            lambda e: e.memset(self.col(self.t_small, 0), 1.0),
            lambda e: e.memset(self.t_onesb[:, :], 1.0),
            lambda e: e.memset(self.col(self.t_small, 24), 0.0),
            lambda e: e.memset(self.t_onesr[:, :], 1.0),
            lambda e: e.memset(self.col(self.t_small, 39), 1.0 / 24.0),
            lambda e: e.memset(self.col(self.t_small, 40), 0.5),
            lambda e: e.memset(self.col(self.t_small, 41), 24.0),
        ]
        for j, t in enumerate([1.0, 3.0, 5.0, 7.0, 10.0, 14.0, 20.0]):
            emits.append(lambda e, j=j, t=t: e.memset(
                self.col(self.t_small, 32 + j), -t))
        for emit in emits:
            s.add("vector", emit, inc="VE")
        vv = s.val("VE")
        s.dma("sync", self.t_pmur[1:2, :], self.t_onesr[0:1, 0:128],
              waits=[("VE", vv)], inc="SY")
        s.dma("sync", self.t_cmov[1:2, :], self.t_onesr[0:1, 0:512],
              waits=[("VE", vv)], inc="SY")
        ey = s.val("SY")
        self._sy_eye = ey
        s.add("vector", lambda e: e.tensor_scalar(
            self.t_eye7[:, :], self.t_eye[:, :], 7.0, None, AL.mult),
            waits=[("SY", ey)], inc="VE")
        s.add("vector", lambda e: e.tensor_scalar(
            self.t_eye15[:, :], self.t_eye[:, :], 15.0, None, AL.mult), inc="VE")
        s.add("vector", lambda e: e.tensor_scalar(
            self.t_eye325[:, :], self.t_eye[:, :], 3.25, None, AL.mult), inc="VE")
        s.add("vector", lambda e: e.tensor_copy(
            self.t_eyeb[:, :], self.t_eye[:, :]), inc="VE")
        s.add("vector", lambda e: e.tensor_scalar(
            self.t_eyeb2[:, :], self.t_eye[:, :], 2.0, None, AL.mult),
            inc="VE")
        s.add("vector", lambda e: e.tensor_scalar(
            self.t_eyeb4[:, :], self.t_eye[:, :], 4.0, None, AL.mult),
            inc="VE")
        s.emit()

    def _qp_mchain(self, s, src, cols, m, waits_v=(), src_c0=0, rows=128,
                   m_c0=0):
        nb = cols // 16
        pstep = src.ap().ap[0][0]
        seg = ap3(src, src_c0, [[pstep, rows], [16, nb], [1, 16]])
        v0 = s.add("vector", lambda e: e.tensor_reduce(
            m[:rows, m_c0:m_c0 + nb], seg, AX, AL.max,
            apply_absolute_value=True), waits=list(waits_v), inc="VE")
        v1 = s.add("vector", lambda e: e.tensor_scalar(
            m[:rows, m_c0:m_c0 + nb], m[:rows, m_c0:m_c0 + nb], 6e-12, None,
            AL.max), waits=[("VE", v0)], inc="VE")
        ve = s.add("vector", lambda e: e.reciprocal(
            m[:rows, m_c0 + nb:m_c0 + 2 * nb], m[:rows, m_c0:m_c0 + nb]),
            waits=[("VE", v1)], inc="VE")
        return ve

    def _qp_core(self, s, src, cols, out, m, mtot_nb, nb0, u, sgb, pc0,
                 wv=(), ws=(), wt=(), src_c0=0, out_c0=0, rows=128):
        nb = cols // 16
        mstep = m.ap().ap[0][0]
        r_bc = ap3(m, mtot_nb + nb0, [[mstep, rows], [1, nb], [0, 16]])
        m2_bc = ap3(m, nb0, [[mstep, rows], [1, nb], [0, 16]])
        c24 = self.t_small[0:rows, 41:42]
        c124 = self.t_small[0:rows, 39:40]
        c05 = self.t_small[0:rows, 40:41]
        vu = s.add("vector", lambda e: e.tensor_tensor(
            u[:rows, 0:cols], src[:rows, src_c0:src_c0 + cols], r_bc,
            AL.mult), waits=list(wv), inc="VE")
        vabs = s.add("scalar", lambda e: e.activation(
            u[:rows, 0:cols], u[:rows, 0:cols], ACT.Abs),
            waits=[("VE", vu)] + list(ws), inc="SE")
        te_ev = {}
        se_ev = {}
        wgt_t = [self.t_eyeb] * 4 + [self.t_eyeb2] * 2 + [self.t_eyeb4]
        nch = (cols + 511) // 512
        for j in range(7):
            slot = sgb[:rows, (j % 2) * cols:(j % 2) * cols + cols]
            w = [("SE", vabs)] if j == 0 else []
            if j >= 2:
                w = [("TE", te_ev[j - 2])]
            se_ev[j] = s.add("scalar", lambda e, slot=slot, j=j: e.activation(
                slot, u[:rows, 0:cols], ACT.Sign, scale=c24,
                bias=self.t_small[0:rows, 32 + j:33 + j]), waits=w, inc="SE")
            for ch in range(nch):
                c0 = ch * 512
                cw = min(512, cols - c0)
                w2 = []
                if ch == 0:
                    w2 = [("SE", se_ev[j])]
                    if j == 0:
                        w2 += list(wt)
                s.add("tensor", lambda e, j=j, c0=c0, cw=cw, slot=slot:
                      e.matmul(
                          self.t_pbig[0:rows, pc0 + c0:pc0 + c0 + cw],
                          wgt_t[j][0:rows, 0:rows],
                          slot[:, c0:c0 + cw] if False else
                          sgb[:rows, (j % 2) * cols + c0:
                              (j % 2) * cols + c0 + cw],
                          start=(j == 0), stop=(j == 6)), waits=w2, inc="TE")
            te_ev[j] = s.val("TE")
        te_stop = te_ev[6]
        se_t = s.add("scalar", lambda e: e.activation(
            u[:rows, 0:cols], self.t_pbig[0:rows, pc0:pc0 + cols],
            ACT.Identity, scale=c124, bias=c05),
            waits=[("TE", te_stop)], inc="SE")
        se_last = s.add("scalar", lambda e: e.activation(
            sgb[:rows, cols:2 * cols], src[:rows, src_c0:src_c0 + cols],
            ACT.Sign), inc="SE")
        vt2 = s.add("vector", lambda e: e.tensor_tensor(
            u[:rows, 0:cols], u[:rows, 0:cols], sgb[:rows, cols:2 * cols],
            AL.mult), waits=[("SE", se_last)], inc="VE")
        ve_out = s.add("vector", lambda e: e.tensor_tensor(
            out[:rows, out_c0:out_c0 + cols], u[:rows, 0:cols], m2_bc,
            AL.mult), waits=[("VE", vt2)], inc="VE")
        return ve_out, te_stop, se_last

    def _quant(self, s, src, rows, cols, out, m, u, sq, a0, ld_wait=None,
               src_c0=0, out_c0=0):
        nb = cols // 16
        pstep = src.ap().ap[0][0]
        mstep = m.ap().ap[0][0]
        seg = ap3(src, src_c0, [[pstep, rows], [16, nb], [1, 16]])
        sv = src[:rows, src_c0:src_c0 + cols]
        ov = out[:rows, out_c0:out_c0 + cols]
        w0 = [("LD", ld_wait)] if ld_wait is not None else []
        s.add("vector", lambda e: e.tensor_reduce(
            m[:rows, 0:nb], seg, AX, AL.max, apply_absolute_value=True),
            waits=w0, inc="VE")
        s.drain("vector")
        s.add("vector", lambda e: e.tensor_scalar(
            m[:rows, 0:nb], m[:rows, 0:nb], 6e-12, None, AL.max), inc="VE")
        s.drain("vector")
        s.add("vector", lambda e: e.reciprocal(
            m[:rows, nb:2 * nb], m[:rows, 0:nb]), inc="VE")
        s.drain("vector")
        s.add("vector", lambda e: e.tensor_scalar(
            m[:rows, 2 * nb:3 * nb], m[:rows, 0:nb], 1.0 / 12.0, None,
            AL.mult), inc="VE")
        s.add("vector", lambda e: e.tensor_scalar(
            m[:rows, nb:2 * nb], m[:rows, nb:2 * nb], 24.0, None, AL.mult),
            inc="VE")
        s.drain("vector")
        r24bc = ap3(m, nb, [[mstep, rows], [1, nb], [0, 16]])
        vu = s.add("vector", lambda e: e.tensor_tensor(
            u[:rows, 0:cols], sv, r24bc, AL.mult), inc="VE")
        squ = s.add("scalar", lambda e: e.square(
            sq[:rows, 0:cols], u[:rows, 0:cols]), waits=[("VE", vu)], inc="SE")
        acc, alt = a0, u
        s.add("vector", lambda e, ac=acc: e.tensor_scalar(
            ac[:rows, 0:cols], sq[:rows, 0:cols], 1.0, None, AL.is_gt),
            waits=[("SE", squ)], inc="VE")
        for rung in RUNGS_U:
            s.drain("vector")
            s.add("vector", lambda e, ac=acc, al=alt, rg=float(rung):
                  e.scalar_tensor_tensor(al[:rows, 0:cols], sq[:rows, 0:cols],
                                         rg, ac[:rows, 0:cols], AL.is_gt,
                                         AL.add), inc="VE")
            acc, alt = alt, acc
        for (rung, wgt) in RUNGS_W:
            s.drain("vector")
            s.add("vector", lambda e, al=alt, rg=float(rung), wg=float(wgt):
                  e.tensor_scalar(al[:rows, 0:cols], sq[:rows, 0:cols], rg, wg,
                                  AL.is_gt, AL.mult), inc="VE")
            s.drain("vector")
            s.add("vector", lambda e, ac=acc, al=alt:
                  e.tensor_tensor(ac[:rows, 0:cols], ac[:rows, 0:cols],
                                  al[:rows, 0:cols], AL.add), inc="VE")
            s.drain("vector")
        lve = s.val("VE")
        sgn = s.add("scalar", lambda e: e.sign(
            sq[:rows, 0:cols], sv), waits=[("VE", lve)], inc="SE")
        s.add("vector", lambda e, ac=acc: e.tensor_tensor(
            ac[:rows, 0:cols], ac[:rows, 0:cols], sq[:rows, 0:cols], AL.mult),
            waits=[("SE", sgn)], inc="VE")
        s.drain("vector")
        dhbc = ap3(m, 2 * nb, [[mstep, rows], [1, nb], [0, 16]])
        return s.add("vector", lambda e, ac=acc: e.tensor_tensor(
            ov, ac[:rows, 0:cols], dhbc, AL.mult), inc="VE")

    def _quant2(self, s, src, cols, out, m, u, sq, a0, sg, waits_v=(),
                src_c0=0, out_c0=0, rows=128):
        nb = cols // 16
        half = min(nb - 1, max(1, (nb * 12) // 25)) * 16
        pstep = src.ap().ap[0][0]
        mstep = m.ap().ap[0][0]
        seg = ap3(src, src_c0, [[pstep, rows], [16, nb], [1, 16]])
        s.add("vector", lambda e: e.tensor_reduce(
            m[:rows, 0:nb], seg, AX, AL.max, apply_absolute_value=True),
            waits=list(waits_v), inc="VE")
        s.drain("vector")
        s.add("vector", lambda e: e.tensor_scalar(
            m[:rows, 0:nb], m[:rows, 0:nb], 6e-12, None, AL.max), inc="VE")
        s.drain("vector")
        s.add("vector", lambda e: e.reciprocal(
            m[:rows, nb:2 * nb], m[:rows, 0:nb]), inc="VE")
        s.drain("vector")
        s.add("vector", lambda e: e.tensor_scalar(
            m[:rows, 2 * nb:3 * nb], m[:rows, 0:nb], 1.0 / 12.0, None,
            AL.mult), inc="VE")
        s.drain("vector")
        s.add("vector", lambda e: e.tensor_scalar(
            m[:rows, nb:2 * nb], m[:rows, nb:2 * nb], 24.0, None, AL.mult),
            inc="VE")
        s.drain("vector")
        r24 = ap3(m, nb, [[mstep, rows], [1, nb], [0, 16]])
        dh = ap3(m, 2 * nb, [[mstep, rows], [1, nb], [0, 16]])
        uev = s.add("vector", lambda e: e.tensor_tensor(
            u[:rows, 0:cols], src[:rows, src_c0:src_c0 + cols], r24, AL.mult),
            inc="VE")
        s.drain("vector")
        squ = s.add("scalar", lambda e: e.square(
            sq[:rows, 0:cols], u[:rows, 0:cols]), waits=[("VE", uev)],
            inc="SE")
        sgn = s.add("scalar", lambda e: e.sign(
            sg[:rows, 0:cols], src[:rows, src_c0:src_c0 + cols]), inc="SE")
        halves = [("vector", "VE", 0, half), ("gpsimd", "GP", half, cols - half)]
        sm_ev = {}
        for (eng, EV, c0h, ch) in halves:
            acc, alt = (a0, u) if eng == "vector" else (u, a0)
            s.add(eng, lambda e, ac=acc, c0h=c0h, ch=ch: e.tensor_scalar(
                ac[:rows, c0h:c0h + ch], sq[:rows, c0h:c0h + ch], 1.0, None,
                AL.is_gt), waits=[("SE", squ)], inc=EV)
            for rung in RUNGS_U:
                s.drain(eng)
                if eng == "vector":
                    s.add(eng, lambda e, ac=acc, al=alt, rg=float(rung),
                          c0h=c0h, ch=ch: e.scalar_tensor_tensor(
                              al[:rows, c0h:c0h + ch], sq[:rows, c0h:c0h + ch],
                              rg, ac[:rows, c0h:c0h + ch], AL.is_gt, AL.add),
                          inc=EV)
                    acc, alt = alt, acc
                else:
                    s.add(eng, lambda e, al=alt, rg=float(rung), c0h=c0h,
                          ch=ch: e.tensor_scalar(
                              al[:rows, c0h:c0h + ch], sq[:rows, c0h:c0h + ch],
                              rg, None, AL.is_gt), inc=EV)
                    s.drain(eng)
                    s.add(eng, lambda e, ac=acc, al=alt, c0h=c0h, ch=ch:
                          e.tensor_tensor(ac[:rows, c0h:c0h + ch],
                                          ac[:rows, c0h:c0h + ch],
                                          al[:rows, c0h:c0h + ch], AL.add),
                          inc=EV)
            for (rung, wgt) in RUNGS_W:
                s.drain(eng)
                s.add(eng, lambda e, al=alt, rg=float(rung), wg=float(wgt),
                      c0h=c0h, ch=ch: e.tensor_scalar(
                          al[:rows, c0h:c0h + ch], sq[:rows, c0h:c0h + ch],
                          rg, wg, AL.is_gt, AL.mult), inc=EV)
                s.drain(eng)
                s.add(eng, lambda e, ac=acc, al=alt, c0h=c0h, ch=ch:
                      e.tensor_tensor(ac[:rows, c0h:c0h + ch],
                                      ac[:rows, c0h:c0h + ch],
                                      al[:rows, c0h:c0h + ch], AL.add), inc=EV)
            s.drain(eng)
            sm_ev[EV] = s.add(eng, lambda e, ac=acc, c0h=c0h, ch=ch:
                              e.tensor_tensor(
                                  ac[:rows, c0h:c0h + ch],
                                  ac[:rows, c0h:c0h + ch],
                                  sg[:rows, c0h:c0h + ch], AL.mult),
                              waits=[("SE", sgn)], inc=EV)
            s.drain(eng)
        fin = s.add("vector", lambda e: e.tensor_tensor(
            out[:rows, out_c0:out_c0 + cols], u[:rows, 0:cols], dh, AL.mult),
            waits=[("GP", sm_ev["GP"])], inc="VE")
        s.drain("vector")
        return fin, sm_ev["GP"]

    def p0_weights(self):
        s = self.s
        self._h16free = {}
        self._biasfree = self.s.val("SY")
        specs = [("wqT", "bq", QSC, "q", "w3", 0),
                 ("wkT", "bk", QSC, "k", "w3", 8),
                 ("wvT", "bv", 1.0, "v", "w3", 16),
                 ("wuT", "bu", 1.0, "u", "h", "wuT16"),
                 ("wgT", "bg", 1.0, "g", "h", "wgT16"),
                 ("bmT", None, 1.0, None, "h", "bmT16"),
                 ("cmT", None, 1.0, None, "h", "cmT16")]
        cvprev = {0: 0, 1: 0}
        teprev = {0: 0, 1: 0}
        for (wname, bname, scale, key, where, slot) in specs:
            for c in range(8):
                buf = self.t_xa if c % 2 == 0 else self.t_xb
                ld = s.dma("scalar", buf[:, :],
                           dr_rows(self.i[wname], c * 128, 128, D),
                           inc="LDA" if c % 2 == 0 else "LDB",
                           waits=[("VE", cvprev[c % 2])])
                if where == "w3":
                    dst = self.t_w3[:, (slot + c) * 1024:(slot + c + 1) * 1024]
                else:
                    st_t = self.t_stg if c % 2 == 0 else self.t_stg2
                    dst = st_t[:, 0:1024]
                stw0 = self._h16free.get(c % 2, 0) if where == "h" else 0
                cv = s.add("vector", lambda e, b=buf, d=dst, sc=float(scale):
                           e.tensor_scalar(d, b[:, :], sc, None, AL.mult),
                           waits=[("LDA" if c % 2 == 0 else "LDB", ld),
                                  ("ST4A" if c % 2 == 0 else "ST4B", stw0),
                                  ("TE", teprev[c % 2])],
                           inc="VE")
                cvprev[c % 2] = cv
                if where == "h":
                    stv = s.dma("gpsimd", dr_rows(self.h[slot], c * 128, 128, D),
                                dst, waits=[("VE", cv)], inc="ST4A" if c % 2 == 0 else "ST4B")
                    self._h16free[c % 2] = stv
                if key is not None:
                    for hf in range(2):
                        s.add("tensor", lambda e, d=dst, c=c, hf=hf: e.matmul(
                            self.pb[6 + hf][0:1, 0:512],
                            self.t_onesb[:, :],
                            d[:, hf * 512:(hf + 1) * 512],
                            start=(c == 0), stop=(c == 7)),
                            waits=[("VE", cv)], inc="TE")
                    teprev[c % 2] = s.val("TE")
            if key is not None:
                te = s.val("TE")
                cst = getattr(self, "t_cs_" + key)
                for hf in range(2):
                    s.add("vector", lambda e, cst=cst, hf=hf: e.tensor_copy(
                        cst[0:1, hf * 512:(hf + 1) * 512],
                        self.pb[6 + hf][0:1, 0:512]), waits=[("TE", te)],
                        inc="VE")
                ldb = s.dma("scalar", self.t_junk[0:1, :], self.i[bname].ap(),
                            inc="LDC", waits=[("SY", self._biasfree)])
                bv = s.add("vector", lambda e, sc=float(scale):
                           e.tensor_scalar(self.t_junk[0:1, :],
                                           self.t_junk[0:1, :], sc, None,
                                           AL.mult), waits=[("LDC", ldb)],
                           inc="VE")
                self._biasfree = s.dma("sync", cst[1:2, :], self.t_junk[0:1, :],
                                       waits=[("VE", bv)], inc="SY")
            s.emit()

        s.dma("scalar", self.t_w1s[:, :], self.i["w1"].ap(), inc="LD")
        s.dma("scalar", self.t_b1s[:, :], self.i["b1"].ap(), inc="LD")
        s.dma("scalar", self.t_w2s[:, :], self.i["w2"].ap(), inc="LD")
        s.dma("scalar", self.t_b2s[:, :],
              AP(self.i["b2"], 0, [[1, 64], [1, 1]]), inc="LD")
        s.dma("scalar", self.t_Drow[:, :], self.i["Dv"].ap(), inc="LD")
        s.dma("scalar", self.t_borow[:, :], self.i["bout"].ap(), inc="LD")
        cvp = 0
        for c in range(8):
            ld = s.dma("scalar", self.t_xa[:, 0:64],
                       dr_rows(self.i["win"], c * 128, 128, 64), inc="LDD",
                       waits=[("VE", cvp)])
            cv = s.add("vector", lambda e, c=c: e.tensor_copy(
                self.t_win16[:, c * 64:(c + 1) * 64], self.t_xa[:, 0:64]),
                waits=[("LDD", ld)], inc="VE")
            cvp = cv
            s.add("tensor", lambda e, c=c: e.matmul(
                self.pb[6][0:1, 0:64], self.t_onesb[:, :],
                self.t_win16[:, c * 64:(c + 1) * 64],
                start=(c == 0), stop=(c == 7)), waits=[("VE", cv)], inc="TE")
        te = s.val("TE")
        s.add("vector", lambda e: e.tensor_copy(
            self.t_cs_in[0:1, 0:64], self.pb[6][0:1, 0:64]),
            waits=[("TE", te)], inc="VE")
        s.dma("scalar", self.t_cs_in[1:2, 0:64], self.i["binf"].ap(), inc="LD")
        ldw = s.dma("scalar", self.t_xa[0:64, :], self.i["wout"].ap(),
                    inc="LDD", waits=[("VE", cvp)])
        s.add("vector", lambda e: e.tensor_copy(
            self.t_wout16[:, :], self.t_xa[0:64, :]), waits=[("LDD", ldw)],
            inc="VE")
        mvp = 0
        for c in range(8):
            ld = s.dma("scalar", self.t_xb[:, 0:128],
                       dr_blk(self.i["Am"], c * 128, 128, c * 128, 128, D),
                       inc="LDE", waits=[("VE", mvp)])
            mvp = s.add("vector", lambda e: e.tensor_tensor(
                self.t_junk[:, 0:128], self.t_xb[:, 0:128], self.t_eye[:, :],
                AL.mult), waits=[("LDE", ld), ("SY", max(self._sy_eye, self._biasfree))], inc="VE")
            s.drain("vector")
            s.add("vector", lambda e, c=c: e.tensor_reduce(
                self.col(self.t_small, 8 + c), self.t_junk[:, 0:128], AX,
                AL.add), inc="VE")
            s.drain("vector")
        ld = s.dma("scalar", self.t_gmax[0:1, 16:24], self.i["cmask"].ap(),
                   inc="LDC")
        te = s.add("tensor", lambda e: e.matmul(
            self.pb[5][0:128, 0:1], self.t_onesr[0:1, 0:128],
            self.t_gmax[0:1, 16:17], start=True, stop=True),
            waits=[("LDC", ld)], inc="TE")
        s.add("vector", lambda e: e.tensor_copy(
            self.col(self.t_small, 2), self.pb[5][0:128, 0:1]),
            waits=[("TE", te)], inc="VE")
        s.add("gpsimd", lambda e: e.nop(),
              waits=[("LD", s.val("LD")), ("LDA", s.val("LDA")),
                     ("LDB", s.val("LDB")), ("LDC", s.val("LDC")),
                     ("LDD", s.val("LDD")), ("LDE", s.val("LDE")),
                     ("ST", s.val("ST")), ("ST4A", s.val("ST4A")),
                     ("ST4B", s.val("ST4B")), ("SY", s.val("SY"))])
        s.emit()

        tprev = 0
        pbev = {6: 0, 7: 0}
        st5_prev = 0
        q_se, q_te, q_ve = 0, 0, 0
        for (src_i, cbuf, nrows, ncols) in [
                ("wg_sh", "wg_in", WSH_GU, D), ("wu_sh", "wu_in", WSH_GU, D),
                ("wd_sh", "wd_in", D, WSH_GU)]:
            nchunks = (nrows + 127) // 128
            for ch in range(nchunks):
                r0 = ch * 128
                nr = min(128, nrows - r0)
                ld = s.dma("scalar", self.t_q4[0:nr, 0:ncols],
                           dr_rows(self.i[src_i], r0, nr, ncols), inc="LD",
                           waits=[("TE", tprev)])
                nbw = ncols // 16
                mv = self._qp_mchain(s, self.t_q4, ncols, self.t_qm,
                                     waits_v=[("LD", ld), ("SE", q_se)],
                                     rows=nr)
                ve, q_te, q_se = self._qp_core(
                    s, self.t_q4, ncols, self.t_q2, self.t_qm, nbw, 0,
                    self.t_q1, self.t_p0sg, 0, wv=[("VE", mv)],
                    ws=[("TE", q_te), ("VE", q_ve)], rows=nr)
                q_ve = ve
                nblk = (ncols + 127) // 128
                for tb in range(nblk):
                    wd = min(128, ncols - tb * 128)
                    bk = 6 + tb % 2
                    te = s.add("tensor", lambda e, tb=tb, nr=nr, wd=wd, bk=bk:
                               e.transpose(self.pb[bk][0:wd, 0:nr],
                                           self.t_q2[0:nr,
                                                     tb * 128:tb * 128 + wd],
                                           self.t_eye[0:nr, 0:nr]),
                               waits=[("VE", ve), ("SE", pbev[bk]),
                                      ("SY", self._sy_eye)], inc="TE")
                    se = s.add("scalar", lambda e, tb=tb, nr=nr, wd=wd, bk=bk:
                               e.copy(self.t_qo[0:wd, tb * 128:tb * 128 + nr],
                                      self.pb[bk][0:wd, 0:nr]),
                               waits=[("TE", te),
                                      ("ST5", st5_prev)], inc="SE")
                    pbev[bk] = se
                    s.dma("gpsimd",
                          dr_blk(self.c[cbuf], tb * 128, wd, r0, nr, nrows),
                          self.t_qo[0:wd, tb * 128:tb * 128 + nr],
                          waits=[("SE", se)], inc="ST5")
                tprev = s.val("TE")
                st5_prev = s.val("ST5")
                s.emit()
        stv = s.val("ST5")
        for (ci, co) in [("wg_in", "wg_out"), ("wu_in", "wu_out"),
                         ("wd_in", "wd_out")]:
            if self.sim1:
                n = self.c[ci].shape[0] * self.c[ci].shape[1]
                nrep = (self.c[co].shape[0] * self.c[co].shape[1]) // n
                for rp in range(nrep):
                    s.add("gpsimd", lambda e, ci=ci, co=co, n=n, rp=rp:
                          e.dma_start(
                              AP(self.c[co], rp * n, [[1, 1], [1, n]]),
                              AP(self.c[ci], 0, [[1, 1], [1, n]])),
                          waits=[("ST5", stv)], inc="AGW", inck=16)
            else:
                s.add("gpsimd", lambda e, ci=ci, co=co: e.collective_compute(
                    "AllGather", AL.bypass, ins=[self.c[ci].ap()],
                    outs=[self.c[co].ap()], replica_groups=[list(range(8))]),
                    waits=[("ST5", stv)], inc="AGW")
        s.emit()
        self._agw = s.val("AGW")
        for dn, cb in [("wgq", "wg_out"), ("wdq", "wd_out")]:
            if dn in self.o_dbg:
                s.dma("gpsimd", self.o_dbg[dn].ap(), self.c[cb].ap(),
                      waits=[("AGW", self._agw)], inc="ST")
                s.emit()


    def _stat_row_store(self, s, src_cols, hrow_off, pbi=5):
        ve = s.val("VE")
        te = s.add("tensor", lambda e: e.transpose(
            self.pb[pbi][0:64, 0:128], src_cols, self.t_eye[:, :]),
            waits=[("VE", ve), ("SE", s.val("SE"))], inc="TE")
        vv = s.add("vector", lambda e: e.tensor_copy(
            self.t_junk[0:64, 0:128], self.pb[pbi][0:64, 0:128]),
            waits=[("TE", te)], inc="VE")
        s.dma("gpsimd", AP(self.h["stats"], hrow_off, [[128, 64], [1, 128]]),
              self.t_junk[0:64, 0:128], waits=[("VE", vv)], inc="ST")

    def p1_stats_xs(self):
        s = self.s
        rdp = {0: 0, 1: 0}
        sqp = {0: 0, 1: 0}
        garb = [self.t_junk[:, :], self.t_q1[:, 0:1024],
                self.t_q2[:, 0:1024], self.t_f_ln4[:, 0:1024]]
        gprev = {0: 0, 1: 0, 2: 0, 3: 0}
        for c in range(64):
            par = c % 2
            buf = self.t_xa if par == 0 else self.t_xb
            sem = "LDA" if par == 0 else "LDB"
            ld = s.dma("scalar", buf[:, :],
                       dr_rows(self.i["x"], c * 128, 128, D), inc=sem,
                       waits=[("VE", rdp[par]), ("SE", sqp[par])])
            rdp[par] = s.add("vector", lambda e, b=buf, c=c: e.tensor_reduce(
                self.col(self.t_cola, c), b[:, :], AX, AL.add),
                waits=[(sem, ld)], inc="VE")
            sqp[par] = s.add("scalar", lambda e, b=buf, c=c: e.activation(
                garb[c % 4], b[:, :], ACT.Square,
                accum_out=self.col(self.t_colb, c)),
                waits=[(sem, ld), ("SE", gprev[c % 4])], inc="SE")
            gprev[c % 4] = sqp[par]
        ve, se0 = s.val("VE"), s.val("SE")
        s.add("vector", lambda e: e.tensor_scalar(
            self.t_cola[:, :], self.t_cola[:, :], 1.0 / D, None, AL.mult),
            waits=[("VE", ve), ("SE", se0)], inc="VE")
        s.drain("vector")
        s.add("vector", lambda e: e.tensor_scalar(
            self.t_colb[:, :], self.t_colb[:, :], 1.0 / D, None, AL.mult),
            inc="VE")
        s.drain("vector")
        s.add("vector", lambda e: e.scalar_tensor_tensor(
            self.t_colc[:, :], self.t_cola[:, :], -1.0, self.t_cola[:, :],
            AL.mult, AL.mult), inc="VE")
        s.drain("vector")
        s.add("vector", lambda e: e.tensor_tensor(
            self.t_colc[:, :], self.t_colc[:, :], self.t_colb[:, :], AL.add),
            inc="VE")
        s.drain("vector")
        s.add("vector", lambda e: e.tensor_scalar(
            self.t_colc[:, :], self.t_colc[:, :], EPS, None, AL.add), inc="VE")
        s.drain("vector")
        vr = s.add("vector", lambda e: e.reciprocal(
            self.t_cold[:, :], self.t_colc[:, :]), inc="VE")
        sq = s.add("scalar", lambda e: e.activation(
            self.t_stats[:, 0:64], self.t_cold[:, :], ACT.Sqrt),
            waits=[("VE", vr)], inc="SE")
        s.add("vector", lambda e: e.scalar_tensor_tensor(
            self.t_stats[:, 64:128], self.t_cola[:, :], -1.0,
            self.t_stats[:, 0:64], AL.mult, AL.mult),
            waits=[("SE", sq)], inc="VE")
        self._stat_row_store(s, self.t_stats[:, 64:128], 1 * L)
        s.emit()

        svp = {0: 0, 1: 0}
        stp = {0: 0, 1: 0}
        for c in range(64):
            par = c % 2
            buf = self.t_xa if par == 0 else self.t_xb
            sem = "LDA" if par == 0 else "LDB"
            ssem = "ST3A" if par == 0 else "ST3B"
            ld = s.dma("scalar", buf[:, :],
                       dr_rows(self.i["x"], c * 128, 128, D), inc=sem,
                       waits=[("SE", svp[par])])
            oap = self.t_xso[:, :] if par == 0 else self.t_p0sg[:, 0:1024]
            sv = s.add("scalar", lambda e, b=buf, c=c, oap=oap: e.activation(
                oap, b[:, :], ACT.Copy,
                scale=self.col(self.t_stats, c)),
                waits=[(sem, ld), (ssem, stp[par]), ("SE", svp[par])],
                inc="SE")
            svp[par] = sv
            stp[par] = s.dma("gpsimd", dr_rows(self.h["xs"], c * 128, 128, D),
                             oap, waits=[("SE", sv)], inc=ssem)
        s.add("gpsimd", lambda e: e.nop(),
              waits=[("ST3A", s.val("ST3A")), ("ST3B", s.val("ST3B")),
                     ("ST", s.val("ST"))])
        s.emit()
        if "xs" in self.o_dbg:
            st = s.val("ST")
            s.dma("gpsimd", self.o_dbg["xs"].ap(), self.h["xs"].ap(),
                  waits=[("ST", st)], inc="ST")
        if "stats" in self.o_dbg:
            st = s.val("ST")
            s.dma("gpsimd", self.o_dbg["stats"].ap(), self.h["stats"].ap(),
                  waits=[("ST", st)], inc="ST")
        s.emit()

    def p1_proj(self):
        s = self.s
        self._bankfree = {}
        self._stgfree = {}
        self._xlm_ts = {}
        wslot = {"Q": 0, "K": 8, "V": 16}
        hdst = {"K": self.h["K"], "V": self.h["V"], "Q": self.h["Q"]}
        csk = {"K": self.t_cs_k, "V": self.t_cs_v, "Q": self.t_cs_q}
        Lc = {"K": L, "V": L, "Q": TLOC}
        for lt in range(16):
            tws = []
            for dc in range(8):
                tw = s.dmaT("sync", self.t_xsblk[:, dc * 512:(dc + 1) * 512],
                            dr_blk(self.h["xs"], lt * 512, 512, dc * 128, 128, D),
                            inc="SY")
                tws.append(tw)
            cm = s.dma("scalar", self.t_cmov[0:1, :],
                       AP(self.h["stats"], L + lt * 512, [[L, 1], [1, 512]]),
                       inc="LD")
            if lt == 0 and "xsb" in self.o_dbg:
                s.dma("gpsimd", self.o_dbg["xsb"].ap(), self.t_xsblk[:, :],
                      waits=[("SY", tws[-1])], inc="ST")
            xstep = self.t_xsblk.ap().ap[0][0]
            rd_ev = {}
            for dc in range(8):
                xseg = ap3(self.t_xsblk, dc * 512, [[xstep, 128], [64, 8], [1, 64]])
                rd_ev[dc] = s.add("vector", lambda e, xseg=xseg, dc=dc:
                                  e.tensor_reduce(
                                      self.t_stats[:, 128 + dc * 8:136 + dc * 8],
                                      xseg, AX, AL.add),
                                  waits=[("SY", tws[-1]),
                                         ("VE", self._xlm_ts.get(dc, 0))],
                                  inc="VE")
            for dc in range(8):
                self._xlm_ts[dc] = s.add(
                    "vector", lambda e, dc=dc, lt=lt: e.tensor_scalar(
                        self.t_Xlm[:, dc * 128 + lt * 8:dc * 128 + lt * 8 + 8],
                        self.t_stats[:, 128 + dc * 8:136 + dc * 8],
                        1.0 / SEG, None, AL.mult),
                    waits=[("VE", rd_ev[dc])], inc="VE")
            for pname in self.projs:
                if pname == "Q" and lt >= 8:
                    continue
                slot, cst, stgt = wslot[pname], csk[pname], (
                    self.t_stg if pname != "V" else self.t_stg2)
                stw = self._stgfree.get(stgt.name, 0)
                stsemw = "ST1" if stgt is self.t_stg else "ST2"
                for fc in range(8):
                    bk = fc % 4
                    bank = self.pb[bk]
                    for dc in range(8):
                        w = ([("SY", tws[-1]), ("LD", cm),
                              ("SE", self._bankfree.get(bk, 0))]
                             if dc == 0 else [])
                        s.add("tensor", lambda e, fc=fc, dc=dc, sl=slot, bank=bank:
                              e.matmul(bank[:, :],
                                       self.t_w3[:, (sl + dc) * 1024 + fc * 128:
                                                 (sl + dc) * 1024 + fc * 128 + 128],
                                       self.t_xsblk[:, dc * 512:(dc + 1) * 512],
                                       start=(dc == 0), stop=False),
                              waits=w, inc="TE")
                    if getattr(self, "no_corr", False):
                        self.s.items[-1] = (self.s.items[-1][0],
                                            (lambda f, b: lambda e: e.matmul(
                                                b[:, :], f[0], f[1],
                                                start=False, stop=True))(
                                                (self.t_w3[:, (slot + 7) * 1024 + fc * 128:
                                                           (slot + 7) * 1024 + fc * 128 + 128],
                                                 self.t_xsblk[:, 7 * 512:8 * 512]),
                                                bank),
                                            self.s.items[-1][2], self.s.items[-1][3])
                    else:
                        s.add("tensor", lambda e, fc=fc, cst=cst, bank=bank: e.matmul(
                            bank[:, :], cst[:, fc * 128:(fc + 1) * 128],
                            self.t_cmov[:, :], start=False, stop=True), inc="TE")
                    tev = s.val("TE")
                    se = s.add("scalar", lambda e, fc=fc, stgt=stgt, bank=bank:
                               e.copy(stgt[:, fc * 512:(fc + 1) * 512],
                                      bank[:, :]),
                               waits=[("TE", tev), (stsemw, stw)], inc="SE")
                    self._bankfree[bk] = se
                sev = s.val("SE")
                dst = AP(hdst[pname], lt * 512,
                         [[Lc[pname], 128], [128 * Lc[pname], 8], [1, 512]])
                stsem = "ST1" if stgt is self.t_stg else "ST2"
                self._stgfree[stgt.name] = s.dma(
                    "gpsimd", dst, stgt[:, :], waits=[("SE", sev)], inc=stsem)
            s.emit()
        s.add("gpsimd", lambda e: e.nop(),
              waits=[("ST", s.val("ST")), ("ST1", s.val("ST1")),
                     ("ST2", s.val("ST2"))])
        if "kk" in self.o_dbg:
            s.dma("gpsimd", self.o_dbg["kk"].ap(), self.h["K"].ap(), inc="ST")
        if "qq" in self.o_dbg:
            s.dma("gpsimd", self.o_dbg["qq"].ap(), self.h["Q"].ap(), inc="ST")
        if "xlm" in self.o_dbg:
            ve = s.val("VE")
            s.dma("gpsimd", self.o_dbg["xlm"].ap(), self.t_Xlm[:, :],
                  waits=[("VE", ve)], inc="ST")
        s.emit()

    def p1_landmarks(self):
        s = self.s
        self._bankfree2 = {}
        self._q1free = {}
        self._gfree = {}
        self._qofree = {}
        self._b2free = {}
        ld = s.dma("scalar", self.t_cola[:, :],
                   AP(self.h["stats"], L, [[64, 128], [1, 64]]), inc="LD")
        rv = s.add("vector", lambda e: e.tensor_reduce(
            self.col(self.t_colb, 0), self.t_cola[:, :], AX, AL.add),
            waits=[("LD", ld)], inc="VE")
        s.drain("vector")
        rv = s.add("vector", lambda e: e.tensor_scalar(
            self.col(self.t_colb, 0), self.col(self.t_colb, 0), 1.0 / SEG,
            None, AL.mult), inc="VE")
        te = s.add("tensor", lambda e: e.transpose(
            self.pb[4][0:1, 0:128], self.col(self.t_colb, 0),
            self.t_eye[:, :]), waits=[("VE", rv)], inc="TE")
        pv = s.add("vector", lambda e: e.tensor_copy(
            self.t_pmur[0:1, :], self.pb[4][0:1, 0:128]),
            waits=[("TE", te)], inc="VE")
        s.emit()

        for (slot, cst, dstt) in [(0, self.t_cs_q, self.t_Qlm),
                                  (8, self.t_cs_k, self.t_Klm)]:
            for fc in range(8):
                bank = self.pb[fc % 4]
                for dc in range(8):
                    w = [("SE", self._bankfree.get(fc % 4, 0))] if dc == 0 else []
                    s.add("tensor", lambda e, fc=fc, dc=dc, sl=slot, bank=bank:
                          e.matmul(bank[0:128, 0:128],
                                   self.t_w3[:, (sl + dc) * 1024 + fc * 128:
                                             (sl + dc) * 1024 + fc * 128 + 128],
                                   self.t_Xlm[:, dc * 128:(dc + 1) * 128],
                                   start=(dc == 0), stop=False),
                          waits=w, inc="TE")
                s.add("tensor", lambda e, fc=fc, cst=cst, bank=bank: e.matmul(
                    bank[0:128, 0:128], cst[:, fc * 128:(fc + 1) * 128],
                    self.t_pmur[:, :], start=False, stop=True), inc="TE")
                tev = s.val("TE")
                se = s.add("scalar", lambda e, fc=fc, d=dstt, bank=bank:
                           e.copy(d[:, fc * 128:(fc + 1) * 128],
                                  bank[0:128, 0:128]),
                           waits=[("TE", tev)], inc="SE")
                self._bankfree[fc % 4] = se
        s.emit()

        for h in range(NH):
            bank = self.pb[h % 2]
            bank2 = self.pb[2 + h % 2]
            rb = (h % 2) * 64
            hc = h // 2
            qs = self.t_Qlm[rb:rb + 64, hc * 128:(hc + 1) * 128]
            ks = self.t_Klm[rb:rb + 64, hc * 128:(hc + 1) * 128]
            te = s.add("tensor", lambda e, qs=qs, ks=ks, bank=bank: e.matmul(
                bank[0:128, 0:128], qs, ks, start=True, stop=True),
                waits=[("SE", self._bankfree2.get(h % 2, 0))], inc="TE")
            ex = s.add("scalar", lambda e, bank=bank: e.activation(
                self.t_e2s[:, :], bank[0:128, 0:128], ACT.Exp,
                accum_out=self.col(self.t_colb, 1)),
                waits=[("TE", te), ("VE", self._q1free.get(0, 0)),
                       ("TE", self._qofree.get(0, 0))], inc="SE")
            self._bankfree2[h % 2] = ex
            rv = s.add("vector", lambda e: e.reciprocal(
                self.col(self.t_colb, 2), self.col(self.t_colb, 1)),
                waits=[("SE", ex)], inc="VE")
            kv = s.add("vector", lambda e, h=h: e.tensor_scalar(
                self.t_k2f32[:, h * 128:(h + 1) * 128], self.t_e2s[:, :],
                self.col(self.t_colb, 2), None, AL.mult),
                waits=[("VE", rv)], inc="VE")
            dv = s.add("vector", lambda e: e.tensor_scalar(
                self.t_dgs[:, :], self.t_eye[:, :],
                self.col(self.t_colb, 2), None, AL.mult),
                waits=[("VE", kv)], inc="VE")
            self._q1free[0] = dv
            tt = s.add("tensor", lambda e, bank2=bank2: e.matmul(
                bank2[0:128, 0:128], self.t_e2s[:, :], self.t_dgs[:, :],
                start=True, stop=True),
                waits=[("VE", dv), ("SE", self._b2free.get(h % 2, 0))],
                inc="TE")
            self._qofree[0] = tt
            k2tc = s.add("scalar", lambda e, h=h, bank2=bank2: e.copy(
                self.t_k2t32[:, h * 128:(h + 1) * 128], bank2[0:128, 0:128]),
                waits=[("TE", tt)], inc="SE")
            self._b2free[h % 2] = k2tc
            cs = s.add("tensor", lambda e, h=h: e.matmul(
                self.pb[5][0:1, 0:128], self.col(self.t_small, 0),
                self.t_k2f32[:, h * 128:(h + 1) * 128], start=True, stop=True),
                waits=[("VE", kv), ("VE", self._gfree.get(0, 0))], inc="TE")
            gm = s.add("vector", lambda e, h=h: e.tensor_reduce(
                self.t_gmax[0:1, h:h + 1], self.pb[5][0:1, 0:128], AX, AL.max),
                waits=[("TE", cs)], inc="VE")
            self._gfree[0] = gm
            if "k2dbg" in self.o_dbg:
                s.add("vector", lambda e, h=h: e.tensor_copy(
                    self.t_k2[:, h * 128:(h + 1) * 128],
                    self.t_k2f32[:, h * 128:(h + 1) * 128]),
                    waits=[("VE", kv)], inc="VE")
        s.emit()

        mv = s.add("vector", lambda e: e.tensor_reduce(
            self.t_gmax[0:1, 16:17], self.t_gmax[0:1, 0:16], AX, AL.max),
            inc="VE")
        s.drain("vector")
        bv = s.add("vector", lambda e: e.tensor_copy(
            self.t_gmax[0:1, 24:32],
            ap3(self.t_gmax, 16, [[32, 1], [0, 8]])), waits=[("VE", mv)],
            inc="VE")
        st = s.dma("gpsimd", self.c["gmax_in"].ap(), self.t_gmax[0:1, 24:32],
                   waits=[("VE", bv)], inc="ST")
        if self.sim1:
            s.add("gpsimd", lambda e: e.dma_start(
                AP(self.c["gmax_out"], 0, [[1, 1], [1, 8]]),
                AP(self.c["gmax_in"], 0, [[1, 1], [1, 8]])),
                waits=[("ST", st)], inc="AGW", inck=16)
        else:
            s.add("gpsimd", lambda e: e.collective_compute(
                "AllReduce", AL.max, ins=[self.c["gmax_in"].ap()],
                outs=[self.c["gmax_out"].ap()],
                replica_groups=[list(range(8))]),
                waits=[("ST", st)], inc="AGW")
        ld = s.dma("scalar", self.t_gmax[0:1, 17:18],
                   AP(self.c["gmax_out"], 0, [[1, 1], [1, 1]]),
                   waits=[("AGW", s.val("AGW"))], inc="LD")
        rv = s.add("vector", lambda e: e.reciprocal(
            self.t_gmax[0:1, 18:19], self.t_gmax[0:1, 17:18]),
            waits=[("LD", s.val("LD"))], inc="VE")
        te = s.add("tensor", lambda e: e.matmul(
            self.pb[5][0:128, 0:1], self.t_onesr[0:1, 0:128],
            self.t_gmax[0:1, 18:19], start=True, stop=True),
            waits=[("VE", rv)], inc="TE")
        gv = s.add("vector", lambda e: e.tensor_copy(
            self.col(self.t_small, 1), self.pb[5][0:128, 0:1]),
            waits=[("TE", te)], inc="VE")
        s.emit()

        s.add("vector", lambda e: e.tensor_scalar(
            self.t_vi32[:, :], self.t_k2t32[:, :], self.col(self.t_small, 1),
            None, AL.mult), inc="VE")
        s.add("vector", lambda e: e.tensor_scalar(
            self.t_vit32[:, :], self.t_k2f32[:, :], self.col(self.t_small, 1),
            None, AL.mult), inc="VE")
        s.emit()

        pA = self.t_pbig[0:128, 0:2048]
        pB = self.t_pbig[0:128, 2048:4096]
        eyerep7 = ap3(self.t_eye7, 0, [[128, 128], [0, NH], [1, 128]])
        eyerep15 = ap3(self.t_eye15, 0, [[128, 128], [0, NH], [1, 128]])
        eyerep325 = ap3(self.t_eye325, 0, [[128, 128], [0, NH], [1, 128]])
        for it in range(6):
            vev = s.val("VE")
            sev = s.val("SE")
            for h in range(NH):
                s.add("tensor", lambda e, h=h: e.matmul(
                    self.t_pbig[0:128, h * 128:(h + 1) * 128],
                    self.t_k2t32[:, h * 128:(h + 1) * 128],
                    self.t_vi32[:, h * 128:(h + 1) * 128],
                    start=True, stop=True),
                    waits=([("VE", vev), ("SE", sev)] if h == 0 else []),
                    inc="TE")
                s.add("tensor", lambda e, h=h: e.matmul(
                    self.t_pbig[0:128, 2048 + h * 128:2048 + (h + 1) * 128],
                    self.t_vi32[:, h * 128:(h + 1) * 128],
                    self.t_k2t32[:, h * 128:(h + 1) * 128],
                    start=True, stop=True), inc="TE")
            t1 = s.val("TE")
            g1 = s.add("vector", lambda e: e.scalar_tensor_tensor(
                self.t_gel32[:, :], pA, -1.0, eyerep7, AL.mult, AL.add),
                waits=[("TE", t1)], inc="VE")
            g1t = s.add("vector", lambda e: e.scalar_tensor_tensor(
                self.t_gelt32[:, :], pB, -1.0, eyerep7, AL.mult, AL.add),
                inc="VE")
            kvt = s.add("scalar", lambda e: e.copy(self.t_kvt32[:, :], pB),
                        waits=[("TE", t1)], inc="SE")
            for h in range(NH):
                s.add("tensor", lambda e, h=h: e.matmul(
                    self.t_pbig[0:128, h * 128:(h + 1) * 128],
                    self.t_kvt32[:, h * 128:(h + 1) * 128],
                    self.t_gel32[:, h * 128:(h + 1) * 128],
                    start=True, stop=True),
                    waits=([("VE", g1t), ("SE", kvt)] if h == 0 else []),
                    inc="TE")
                s.add("tensor", lambda e, h=h: e.matmul(
                    self.t_pbig[0:128, 2048 + h * 128:2048 + (h + 1) * 128],
                    self.t_gel32[:, h * 128:(h + 1) * 128],
                    self.t_kvt32[:, h * 128:(h + 1) * 128],
                    start=True, stop=True), inc="TE")
            t2 = s.val("TE")
            g2 = s.add("vector", lambda e: e.scalar_tensor_tensor(
                self.t_gel32[:, :], pA, -1.0, eyerep15, AL.mult, AL.add),
                waits=[("TE", t2)], inc="VE")
            g2t = s.add("vector", lambda e: e.scalar_tensor_tensor(
                self.t_gelt32[:, :], pB, -1.0, eyerep15, AL.mult, AL.add),
                inc="VE")
            for h in range(NH):
                s.add("tensor", lambda e, h=h: e.matmul(
                    self.t_pbig[0:128, h * 128:(h + 1) * 128],
                    self.t_kvt32[:, h * 128:(h + 1) * 128],
                    self.t_gel32[:, h * 128:(h + 1) * 128],
                    start=True, stop=True),
                    waits=([("VE", g2t)] if h == 0 else []), inc="TE")
                s.add("tensor", lambda e, h=h: e.matmul(
                    self.t_pbig[0:128, 2048 + h * 128:2048 + (h + 1) * 128],
                    self.t_gel32[:, h * 128:(h + 1) * 128],
                    self.t_kvt32[:, h * 128:(h + 1) * 128],
                    start=True, stop=True), inc="TE")
            t3 = s.val("TE")
            g3 = s.add("vector", lambda e: e.scalar_tensor_tensor(
                self.t_gel32[:, :], pA, -0.25, eyerep325, AL.mult, AL.add),
                waits=[("TE", t3)], inc="VE")
            g3t = s.add("vector", lambda e: e.scalar_tensor_tensor(
                self.t_gelt32[:, :], pB, -0.25, eyerep325, AL.mult, AL.add),
                inc="VE")
            for h in range(NH):
                s.add("tensor", lambda e, h=h: e.matmul(
                    self.t_pbig[0:128, h * 128:(h + 1) * 128],
                    self.t_vit32[:, h * 128:(h + 1) * 128],
                    self.t_gel32[:, h * 128:(h + 1) * 128],
                    start=True, stop=True),
                    waits=([("VE", g3t)] if h == 0 else []), inc="TE")
                s.add("tensor", lambda e, h=h: e.matmul(
                    self.t_pbig[0:128, 2048 + h * 128:2048 + (h + 1) * 128],
                    self.t_gel32[:, h * 128:(h + 1) * 128],
                    self.t_vit32[:, h * 128:(h + 1) * 128],
                    start=True, stop=True), inc="TE")
            t4 = s.val("TE")
            s.add("scalar", lambda e: e.copy(self.t_vi32[:, :], pA),
                  waits=[("TE", t4)], inc="SE")
            s.add("scalar", lambda e: e.copy(self.t_vit32[:, :], pB), inc="SE")
            s.emit()
        s.add("vector", lambda e: e.tensor_copy(
            self.t_ViT[:, :], self.t_vit32[:, :]), inc="VE")
        s.emit()
        if "k2dbg" in self.o_dbg:
            s.dma("gpsimd", self.o_dbg["k2dbg"].ap(), self.t_k2[:, :], inc="ST")
        if "invT" in self.o_dbg:
            s.dma("gpsimd", self.o_dbg["invT"].ap(), self.t_ViT[:, :], inc="ST")
        s.add("gpsimd", lambda e: e.nop(), waits=[("ST", s.val("ST"))])
        s.emit()

    def p1_pairs(self):
        s = self.s
        self._pb_ev = {}
        self._pv_ev = {}
        self._m2_ev = {}
        self._g_ev = {}
        pe = {"k2rd": 0, "vtrd": 0, "q2rd": 0, "p3wr": 0, "p1wr": 0,
              "attst": 0, "m2rd": 0}
        vstep = self.t_Vt.ap().ap[0][0]
        s.add("vector", lambda e: e.memset(
            ap3(self.t_Vt, 64, [[vstep, 128], [160, 64], [80, 2], [1, 1]]),
            1.0), inc="VE")
        s.add("vector", lambda e: e.memset(
            ap3(self.t_M2, 64, [[self.t_M2.ap().ap[0][0], 128], [65, 2], [1, 1]]),
            1.0), inc="VE")
        s.emit()
        for hp in range(8):
            ldk = s.dma("scalar", self.t_K2[:, :],
                        dr_rows(self.h["K"], hp * 128, 128, L), inc="LKA",
                        waits=[("TE", pe["k2rd"])])
            ldq = s.dma("scalar", self.t_Q2[:, :],
                        dr_rows(self.h["Q"], hp * 128, 128, TLOC), inc="LQA",
                        waits=[("TE", pe["q2rd"])])
            ldv = 0
            for hl in range(2):
                vdst = ap3(self.t_Vt, hl * 80,
                           [[vstep, 128], [160, 64], [1, 64]])
                ldv = s.dmaT("sync", vdst,
                             dr_rows(self.h["V"], hp * 128 + hl * 64, 64, L),
                             waits=[("TE", pe["vtrd"])], inc="SY")
            for g in range(16):
                for hl in range(2):
                    bank = self.pb[(g % 2) * 2 + hl]
                    for j in range(4):
                        lc = g * 4 + j
                        w = [("LKA", ldk), ("SE", self._pb_ev.get((g % 2) * 2 + hl, 0))] \
                            if j == 0 else []
                        if g == 0 and hl == 0 and j == 0:
                            w.append(("VE", s.val("VE")))
                        s.add("tensor", lambda e, hl=hl, lc=lc, j=j, bank=bank:
                              e.matmul(bank[0:128, j * 128:(j + 1) * 128],
                                       self.t_K2[hl * 64:hl * 64 + 64,
                                                 lc * 128:(lc + 1) * 128],
                                       self.t_Qlm[hl * 64:hl * 64 + 64,
                                                  hp * 128:(hp + 1) * 128],
                                       start=True, stop=True),
                              waits=w, inc="TE")
                    tev = s.val("TE")
                    ev = s.add("scalar", lambda e, hl=hl, g=g, bank=bank:
                               e.activation(
                                   self.t_P3T[:, hl * L + g * 512:
                                              hl * L + (g + 1) * 512],
                                   bank[0:128, 0:512], ACT.Exp),
                               waits=[("TE", tev), ("TE", pe["p3wr"])],
                               inc="SE")
                    self._pb_ev[(g % 2) * 2 + hl] = ev
            sev_p3 = s.val("SE")
            for hl in range(2):
                for lc in range(64):
                    w = []
                    if lc == 0:
                        w = [("SE", sev_p3), ("SY", ldv),
                             ("VE", self._pv_ev.get(hl, 0))]
                    s.add("tensor", lambda e, hl=hl, lc=lc: e.matmul(
                        self.pb[4 + hl][0:128, 0:65],
                        self.t_P3T[:, hl * L + lc * 128:hl * L + (lc + 1) * 128],
                        ap3(self.t_Vt, lc * 160 + hl * 80,
                            [[vstep, 128], [1, 65]]),
                        start=(lc == 0), stop=(lc == 63)), waits=w, inc="TE")
                tpv = s.val("TE")
                rc = s.add("vector", lambda e, hl=hl: e.reciprocal(
                    self.col(self.t_colb, 4 + hl),
                    self.pb[4 + hl][0:128, 64:65]), waits=[("TE", tpv)],
                    inc="VE")
                s.drain("vector")
                pv = s.add("vector", lambda e, hl=hl: e.tensor_scalar(
                    self.t_PVn[:, hl * 64:(hl + 1) * 64],
                    self.pb[4 + hl][0:128, 0:64],
                    self.col(self.t_colb, 4 + hl), None, AL.mult), inc="VE")
                self._pv_ev[hl] = pv
                h = 2 * hp + hl
                tm = s.add("tensor", lambda e, h=h, hl=hl: e.matmul(
                    self.pb[6][0:128, 0:64],
                    self.t_ViT[:, h * 128:(h + 1) * 128],
                    self.t_PVn[:, hl * 64:(hl + 1) * 64],
                    start=True, stop=True),
                    waits=[("VE", pv), ("SE", self._m2_ev.get(0, 0)),
                           ("TE", pe["m2rd"])], inc="TE")
                m2 = s.add("scalar", lambda e, hl=hl: e.copy(
                    self.t_M2[:, hl * 65:hl * 65 + 64], self.pb[6][0:128, 0:64]),
                    waits=[("TE", tm)], inc="SE")
                self._m2_ev[0] = m2
            sev_m2 = s.val("SE")
            for hl in range(2):
                for sc in range(8):
                    bank = self.pb[(sc % 2) * 2 + hl]
                    w = [("LQA", ldq),
                         ("SE", self._pb_ev.get((sc % 2) * 2 + hl, 0))]
                    s.add("tensor", lambda e, hl=hl, sc=sc, bank=bank: e.matmul(
                        bank[0:128, 0:512],
                        self.t_Klm[hl * 64:hl * 64 + 64,
                                   hp * 128:(hp + 1) * 128],
                        self.t_Q2[hl * 64:hl * 64 + 64,
                                  sc * 512:(sc + 1) * 512],
                        start=True, stop=True), waits=w, inc="TE")
                    tev = s.val("TE")
                    ev = s.add("scalar", lambda e, hl=hl, sc=sc, bank=bank:
                               e.activation(
                                   self.t_P1[:, hl * TLOC + sc * 512:
                                             hl * TLOC + (sc + 1) * 512],
                                   bank[0:128, 0:512], ACT.Exp),
                               waits=[("TE", tev), ("TE", pe["p1wr"])],
                               inc="SE")
                    self._pb_ev[(sc % 2) * 2 + hl] = ev
            sev_p1 = s.val("SE")
            for lc2 in range(32):
                bank = self.pb[6 + lc2 % 2]
                for hl in range(2):
                    if hl == 0:
                        pse, pve = self._g_ev.get(6 + lc2 % 2, (0, 0))
                        w = [("SE", max(sev_p1 if lc2 == 0 else 0, pse)),
                             ("VE", pve)]
                    else:
                        w = []
                    if lc2 == 0 and hl == 0:
                        w.append(("SE", sev_m2))
                    s.add("tensor", lambda e, hl=hl, lc2=lc2, bank=bank:
                          e.matmul(bank[0:128, hl * 65:hl * 65 + 65],
                                   self.t_P1[:, hl * TLOC + lc2 * 128:
                                             hl * TLOC + (lc2 + 1) * 128],
                                   self.t_M2[:, hl * 65:(hl + 1) * 65],
                                   start=True, stop=True), waits=w, inc="TE")
                tev = s.val("TE")
                pstep = self.t_att.ap().ap[0][0]
                nsrc = ap3(bank.tensor if False else self.t_pbig,
                           (6 + lc2 % 2) * 512,
                           [[4096, 128], [65, 2], [1, 64]])
                ne = s.add("scalar", lambda e, lc2=lc2, nsrc=nsrc: e.copy(
                    ap3(self.t_att, lc2 * 130,
                        [[pstep, 128], [64, 2], [1, 64]]), nsrc),
                    waits=[("TE", tev)], inc="SE")
                dsrc = ap3(self.t_pbig, (6 + lc2 % 2) * 512 + 64,
                           [[4096, 128], [65, 2], [1, 1]])
                de = s.add("vector", lambda e, lc2=lc2, dsrc=dsrc: e.tensor_copy(
                    ap3(self.t_att, lc2 * 130 + 128,
                        [[pstep, 128], [1, 2], [1, 1]]), dsrc),
                    waits=[("TE", tev)], inc="VE")
                self._g_ev[6 + lc2 % 2] = (ne, de)
            sev_g = s.val("SE")
            vev_g = s.val("VE")
            rd = s.add("vector", lambda e: e.reciprocal(
                self.t_rden[:, :],
                ap3(self.t_att, 128, [[pstep, 128], [130, 32], [1, 2]])),
                waits=[("VE", vev_g), ("SE", sev_g), ("STA", pe["attst"])],
                inc="VE")
            s.drain("vector")
            astep = self.t_atn.ap().ap[0][0]
            nv = s.add("vector", lambda e: e.tensor_tensor(
                ap3(self.t_atn, 0, [[astep, 128], [128, 32], [64, 2], [1, 64]]),
                ap3(self.t_att, 0, [[pstep, 128], [130, 32], [64, 2], [1, 64]]),
                ap3(self.t_rden, 0,
                    [[self.t_rden.ap().ap[0][0], 128], [2, 32], [1, 2], [0, 64]]),
                AL.mult), inc="VE")
            adst = AP(self.h["attn"], hp * 128,
                      [[D, 128], [128 * D, 32], [1, 128]])
            pe["attst"] = s.dma("gpsimd", adst, self.t_atn[:, :],
                                waits=[("VE", nv)], inc="STA")
            if hp == 0 and any(dn in self.o_dbg for dn in
                               ("p3t", "pvn", "m2d", "p1d1", "attd")):
                for dn, dt_ in [("p3t", self.t_P3T), ("pvn", self.t_PVn),
                                ("vtd", self.t_Vt),
                                ("m2d", self.t_M2), ("p1d1", self.t_P1),
                                ("attd", self.t_att)]:
                    if dn in self.o_dbg:
                        s.dma("gpsimd", self.o_dbg[dn].ap(), dt_[:, :],
                              waits=[("VE", nv), ("SE", s.val("SE"))],
                              inc="ST")
                s.add("gpsimd", lambda e: e.nop(), waits=[("ST", s.val("ST"))])
                s.emit()
            pe["k2rd"] = s.val("TE")
            pe["q2rd"] = s.val("TE")
            pe["vtrd"] = s.val("TE")
            pe["m2rd"] = s.val("TE")
            pe["p3wr"] = s.val("TE")
            pe["p1wr"] = s.val("TE")
            s.emit()
        s.add("gpsimd", lambda e: e.nop(), waits=[("STA", s.val("STA"))])
        s.emit()

    def p1_x1(self):
        s = self.s
        lastse = {0: 0, 1: 0}
        lastve = {0: 0, 1: 0}
        stp = {0: 0, 1: 0}
        for c in range(32):
            par = c % 2
            xbuf = self.t_xa if par == 0 else self.t_xb
            abuf = self.t_q1 if par == 0 else self.t_q2
            lsem = "LXA" if par == 0 else "LXB"
            asem = "LAA" if par == 0 else "LAB"
            ssem = "STRA" if par == 0 else "STRB"
            lx = s.dma("scalar", xbuf[:, :],
                       dr_rows(self.i["x"], c * 128, 128, D), inc=lsem,
                       waits=[("SE", lastse[par]), ("VE", lastve[par]),
                              (ssem, stp[par])])
            la = s.dma("sync", abuf[:, 0:1024],
                       dr_rows(self.h["attn"], c * 128, 128, D), inc=asem,
                       waits=[("VE", lastve[par]), ("SE", lastse[par])])
            vv = s.add("vector", lambda e, xb=xbuf, ab=abuf, c=c:
                       e.scalar_tensor_tensor(
                           xb[:, :], xb[:, :], 1.0, ab[:, 0:1024], AL.mult,
                           AL.add, accum_out=self.col(self.t_cola, c)),
                       waits=[(lsem, lx), (asem, la)], inc="VE")
            lastve[par] = vv
            sqa = s.add("scalar", lambda e, xb=xbuf, ab=abuf, c=c:
                        e.activation(ab[:, 0:1024], xb[:, :], ACT.Square,
                                     accum_out=self.col(self.t_colb, c)),
                        waits=[("VE", vv)], inc="SE")
            lastse[par] = sqa
            stp[par] = s.dma("gpsimd", dr_rows(self.h["res"], c * 128, 128, D),
                             xbuf[:, :], waits=[("VE", vv)], inc=ssem)
        ve0, se0 = s.val("VE"), s.val("SE")
        v1 = s.add("vector", lambda e: e.tensor_scalar(
            self.t_cola[:, 0:32], self.t_cola[:, 0:32], 1.0 / D, None,
            AL.mult), waits=[("VE", ve0), ("SE", se0)], inc="VE")
        v2 = s.add("vector", lambda e: e.tensor_scalar(
            self.t_colb[:, 0:32], self.t_colb[:, 0:32], 1.0 / D, None,
            AL.mult), waits=[("VE", v1)], inc="VE")
        v3 = s.add("vector", lambda e: e.scalar_tensor_tensor(
            self.t_colc[:, 0:32], self.t_cola[:, 0:32], -1.0,
            self.t_cola[:, 0:32], AL.mult, AL.mult), waits=[("VE", v2)],
            inc="VE")
        v4 = s.add("vector", lambda e: e.tensor_tensor(
            self.t_colc[:, 0:32], self.t_colc[:, 0:32], self.t_colb[:, 0:32],
            AL.add), waits=[("VE", v3)], inc="VE")
        v5 = s.add("vector", lambda e: e.tensor_scalar(
            self.t_colc[:, 0:32], self.t_colc[:, 0:32], EPS, None, AL.add),
            waits=[("VE", v4)], inc="VE")
        v6 = s.add("vector", lambda e: e.reciprocal(
            self.t_cold[:, 0:32], self.t_colc[:, 0:32]), waits=[("VE", v5)],
            inc="VE")
        sq = s.add("scalar", lambda e: e.activation(
            self.t_stats[:, 0:32], self.t_cold[:, 0:32], ACT.Sqrt),
            waits=[("VE", v6)], inc="SE")
        nm = s.add("vector", lambda e: e.scalar_tensor_tensor(
            self.t_stats[:, 32:64], self.t_cola[:, 0:32], -1.0,
            self.t_stats[:, 0:32], AL.mult, AL.mult), waits=[("SE", sq)],
            inc="VE")
        s.emit()
        bufs = [self.t_xa, self.t_xb, self.t_q1, self.t_q2]
        ln_ev = {}
        q_se, q_te, q_ve, qln = 0, 0, 0, 0
        for b4 in range(8):
            lds = []
            for i in range(4):
                c = b4 * 4 + i
                ld = s.dma("scalar", bufs[i][:, 0:1024],
                           dr_rows(self.h["res"], c * 128, 128, D), inc="LXC",
                           waits=[("SE", ln_ev.get(i, 0))] +
                           ([("STRA", s.val("STRA")),
                             ("STRB", s.val("STRB"))] if b4 == 0 and i == 0
                            else []))
                lds.append(ld)
            for i in range(4):
                c = b4 * 4 + i
                w = [("LXC", lds[3])]
                if i == 0:
                    w += [("VE", q_ve), ("VE", nm)]
                ln_ev[i] = s.add("scalar", lambda e, i=i, c=c: e.activation(
                    self.t_f_ln4[:, i * 1024:(i + 1) * 1024],
                    bufs[i][:, 0:1024], ACT.Identity,
                    scale=self.col(self.t_stats, c),
                    bias=self.col(self.t_stats, 32 + c)), waits=w, inc="SE")
            mv = self._qp_mchain(s, self.t_f_ln4, 4096, self.t_f_m,
                                 waits_v=[("SE", ln_ev[3]), ("SE", q_se)])
            ve_o, te_o, se_o = self._qp_core(
                s, self.t_f_ln4, 4096, self.t_f_qo, self.t_f_m, 256, 0,
                self.t_f_u, self.t_f_sgb, 0,
                wv=[("SE", q_se), ("QLN", qln), ("VE", mv)],
                ws=[("TE", q_te), ("VE", q_ve)], wt=[("SE", q_se)])
            q_se, q_te, q_ve = se_o, te_o, ve_o
            qln = s.dma("gpsimd",
                        AP(self.h["qlnx"], b4 * 512 * D,
                           [[D, 128], [128 * D, 4], [1, 1024]]),
                        self.t_f_qo[:, :], waits=[("VE", ve_o)], inc="QLN")
        s.add("gpsimd", lambda e: e.nop(),
              waits=[("STRA", s.val("STRA")), ("STRB", s.val("STRB")),
                     ("QLN", s.val("QLN"))])
        if "attn" in self.o_dbg:
            s.dma("gpsimd", self.o_dbg["attn"].ap(), self.h["attn"].ap(),
                  inc="ST")
        if "x1" in self.o_dbg:
            s.dma("gpsimd", self.o_dbg["x1"].ap(), self.h["res"].ap(), inc="ST")
        if "qlnx" in self.o_dbg:
            s.dma("gpsimd", self.o_dbg["qlnx"].ap(), self.h["qlnx"].ap(),
                  inc="ST")
        s.add("gpsimd", lambda e: e.nop(), waits=[("ST", s.val("ST"))])
        s.emit()

    def p2_ffn(self):
        s = self.s
        self._agw = s.val("AGW")
        bank_ev = {}
        qlx_war, qhf_war = 0, 0
        wsl_war = {0: 0, 1: 0}
        wd_war = {0: 0, 1: 0}
        sgs_war, hv_war = 0, 0
        x4_war, x4_ve = 0, 0
        q_se, q_te, q_ve, stq = 0, 0, 0, 0
        for g in range(8):
            qlx = 0
            for dc in range(8):
                qlx = s.dmaT("sync", self.t_qlxf[:, dc * 512:(dc + 1) * 512],
                             dr_blk(self.h["qlnx"], g * 512, 512, dc * 128,
                                    128, D),
                             inc="QLX", waits=[("TE", qlx_war)])
            wsl_ld = {}

            def _wsl_load(sh, g=g):
                par = sh % 2
                base = par * 16 * WSH_GU
                ev = 0
                for gu in range(2):
                    cout = self.c["wg_out"] if gu == 0 else self.c["wu_out"]
                    for dc in range(8):
                        w = [("TE", wsl_war[par])] if (gu == 0 and dc == 0) \
                            else []
                        if g == 0 and sh == 0 and gu == 0 and dc == 0:
                            w.append(("AGW", self._agw))
                        ev = s.dma("gpsimd",
                                   self.t_wsl[:, base + (gu * 8 + dc) * WSH_GU:
                                              base + (gu * 8 + dc + 1) * WSH_GU],
                                   dr_rows(cout, sh * 1024 + dc * 128, 128,
                                           WSH_GU),
                                   inc="WSA" if par == 0 else "WSB", waits=w)
                wsl_ld[sh] = ev

            _wsl_load(0)
            _wsl_load(1)
            for sh in range(8):
                par = sh % 2
                base = par * 16 * WSH_GU
                wsem = "WSA" if par == 0 else "WSB"
                for dc in range(8):
                    for tc in range(4):
                        lhs = self.t_qlxf[:, dc * 512 + tc * 128:
                                          dc * 512 + tc * 128 + 128]
                        for gu in range(2):
                            b = gu * 4 + tc
                            w = []
                            if dc == 0:
                                w = bank_ev.pop(b, [])
                                if tc == 0 and gu == 0:
                                    w += [("QLX", qlx), (wsem, wsl_ld[sh])]
                            s.add("tensor", lambda e, lhs=lhs, b=b, gu=gu,
                                  dc=dc, base=base: e.matmul(
                                      self.pb[b][0:128, 0:WSH_GU], lhs,
                                      self.t_wsl[:, base + (gu * 8 + dc) * WSH_GU:
                                                 base + (gu * 8 + dc + 1) * WSH_GU],
                                      start=(dc == 0), stop=(dc == 7)),
                                  waits=w, inc="TE")
                wsl_war[par] = s.val("TE")
                te_sh = s.val("TE")
                if sh + 2 < 8:
                    _wsl_load(sh + 2)
                for tc in range(4):
                    bg, bu = self.pb[tc], self.pb[4 + tc]
                    sg1 = s.add("scalar", lambda e, bg=bg: e.activation(
                        self.t_sgs[:, 0:WSH_GU], bg[0:128, 0:WSH_GU],
                        ACT.Sigmoid),
                        waits=[("TE", te_sh), ("VE", sgs_war)], inc="SE")
                    tm = s.add("vector", lambda e, bg=bg: e.tensor_tensor(
                        self.t_sgs[:, 512:512 + WSH_GU], bg[0:128, 0:WSH_GU],
                        self.t_sgs[:, 0:WSH_GU], AL.mult),
                        waits=[("SE", sg1), ("VE", hv_war)], inc="VE")
                    sgs_war = tm
                    hv = s.add("vector", lambda e, bu=bu, tc=tc, sh=sh:
                               e.tensor_tensor(
                                   self.t_h4[:, tc * IFF + sh * WSH_GU:
                                             tc * IFF + (sh + 1) * WSH_GU],
                                   self.t_sgs[:, 512:512 + WSH_GU],
                                   bu[0:128, 0:WSH_GU], AL.mult),
                               waits=[("VE", tm)], inc="VE")
                    hv_war = hv
                    bank_ev[tc] = [("VE", tm), ("SE", sg1)]
                    bank_ev[4 + tc] = [("VE", hv)]
            qlx_war = s.val("TE")
            mv = self._qp_mchain(s, self.t_h4, 4 * IFF, self.t_p2m,
                                 waits_v=[("SE", q_se), ("VE", hv_war)])
            for tc in range(4):
                wv = [("SE", q_se), ("VE", mv), ("VE", q_ve), ("ST6", stq)]
                if tc == 0:
                    wt = []
                    for bnk in range(6):
                        wt += bank_ev.pop(bnk, [])
                else:
                    wt = [("SE", q_se)]
                ve_o, te_o, se_o = self._qp_core(
                    s, self.t_h4, IFF, self.t_p2o, self.t_p2m, 704, tc * 176,
                    self.t_p2u, self.t_p2sg, 0,
                    wv=wv, ws=[("TE", q_te), ("VE", q_ve)], wt=wt,
                    src_c0=tc * IFF)
                q_se, q_te, q_ve = se_o, te_o, ve_o
                stq = s.dma("gpsimd",
                            dr_rows(self.h["qh"], g * 512 + tc * 128, 128,
                                    IFF),
                            self.t_p2o[:, :],
                            waits=[("VE", ve_o)], inc="ST6")
            for bnk in range(6):
                bank_ev[bnk] = [("SE", q_se)]
            qf = 0
            for ic in range(22):
                w = [("TE", qhf_war)]
                if ic == 0:
                    w.append(("ST6", stq))
                qf = s.dmaT("sync", self.t_qhf[:, ic * 512:(ic + 1) * 512],
                            dr_blk(self.h["qh"], g * 512, 512, ic * 128, 128,
                                   IFF), inc="QHF", waits=w)
            for ic in range(22):
                par = ic % 2
                wdsem = "WDA" if par == 0 else "WDB"
                wld = [("TE", wd_war[par])]
                if g == 0 and ic == 0:
                    wld.append(("AGW", self._agw))
                ldw = s.dma("scalar",
                            self.t_wd2[:, par * 1024:(par + 1) * 1024],
                            dr_rows(self.c["wd_out"], ic * 128, 128, D),
                            inc=wdsem, waits=wld)
                for tc in range(4):
                    lhs = self.t_qhf[:, ic * 512 + tc * 128:
                                     ic * 512 + tc * 128 + 128]
                    for dsl in range(2):
                        b = tc * 2 + dsl
                        w = []
                        if ic == 0:
                            w = bank_ev.pop(b, [])
                            if tc == 0 and dsl == 0:
                                w.append(("QHF", qf))
                        if tc == 0 and dsl == 0:
                            w.append((wdsem, ldw))
                        s.add("tensor", lambda e, lhs=lhs, b=b, par=par,
                              dsl=dsl, ic=ic: e.matmul(
                                  self.pb[b][0:128, 0:512], lhs,
                                  self.t_wd2[:, par * 1024 + dsl * 512:
                                             par * 1024 + dsl * 512 + 512],
                                  start=(ic == 0), stop=(ic == 21)),
                              waits=w, inc="TE")
                wd_war[par] = s.val("TE")
            qhf_war = s.val("TE")
            te_d = s.val("TE")
            for th in range(2):
                lxr = 0
                for i in range(2):
                    tc = th * 2 + i
                    w = [("STR", x4_war), ("VE", x4_ve)] if i == 0 else []
                    lxr = s.dma("sync",
                                self.t_x4t[:, i * 1024:(i + 1) * 1024],
                                dr_rows(self.h["res"], g * 512 + tc * 128,
                                        128, D), inc="LXR", waits=w)
                for i in range(2):
                    tc = th * 2 + i
                    ev = 0
                    for dsl in range(2):
                        b = tc * 2 + dsl
                        w = [("TE", te_d), ("LXR", lxr)] if (i == 0 and
                                                             dsl == 0) else []
                        ev = s.add("vector", lambda e, b=b, i=i, dsl=dsl:
                                   e.tensor_tensor(
                                       self.t_x4t[:, i * 1024 + dsl * 512:
                                                  i * 1024 + dsl * 512 + 512],
                                       self.t_x4t[:, i * 1024 + dsl * 512:
                                                  i * 1024 + dsl * 512 + 512],
                                       self.pb[b][0:128, 0:512], AL.add),
                                   waits=w, inc="VE")
                        bank_ev[b] = [("VE", ev)]
                    s.dma("gpsimd",
                          dr_rows(self.h["res"], g * 512 + tc * 128, 128, D),
                          self.t_x4t[:, i * 1024:(i + 1) * 1024],
                          waits=[("VE", ev)], inc="STR")
                x4_war = s.val("STR")
                x4_ve = s.val("VE")
        s.add("gpsimd", lambda e: e.nop(),
              waits=[("STR", s.val("STR")), ("ST6", s.val("ST6"))])
        if "qh" in self.o_dbg:
            s.dma("gpsimd", self.o_dbg["qh"].ap(), self.h["qh"].ap(), inc="ST")
        if "x2" in self.o_dbg:
            s.dma("gpsimd", self.o_dbg["x2"].ap(), self.h["res"].ap(),
                  inc="ST")
        s.add("gpsimd", lambda e: e.nop(), waits=[("ST", s.val("ST"))])
        s.emit()

    def _stats_ln(self, s, xbuf, par, out_ap, ld_waits):
        cmu = self.col(self.t_cola, par)
        csq = self.col(self.t_colb, par)
        cvr = self.col(self.t_colc, par)
        crr = self.col(self.t_cold, par)
        crr2 = self.col(self.t_cold, 2 + par)
        cnm = self.col(self.t_cola, 2 + par)
        s.drain("vector")
        s.add("vector", lambda e, xb=xbuf, cmu=cmu: e.tensor_reduce(
            cmu, xb[:, :], AX, AL.add), waits=list(ld_waits), inc="VE")
        sqa = s.add("scalar", lambda e, xb=xbuf, csq=csq: e.activation(
            self.t_junk[:, :], xb[:, :], ACT.Square, accum_out=csq),
            waits=list(ld_waits), inc="SE")
        s.drain("vector")
        s.add("vector", lambda e, cmu=cmu: e.tensor_scalar(
            cmu, cmu, 1.0 / D, None, AL.mult), inc="VE")
        s.drain("vector")
        s.add("vector", lambda e, csq=csq: e.tensor_scalar(
            csq, csq, 1.0 / D, None, AL.mult), waits=[("SE", sqa)], inc="VE")
        s.drain("vector")
        s.add("vector", lambda e, cmu=cmu, cvr=cvr: e.scalar_tensor_tensor(
            cvr, cmu, -1.0, cmu, AL.mult, AL.mult), inc="VE")
        s.drain("vector")
        s.add("vector", lambda e, cvr=cvr, csq=csq: e.tensor_tensor(
            cvr, cvr, csq, AL.add), inc="VE")
        s.drain("vector")
        s.add("vector", lambda e, cvr=cvr: e.tensor_scalar(
            cvr, cvr, EPS, None, AL.add), inc="VE")
        s.drain("vector")
        rcv = s.add("vector", lambda e, cvr=cvr, crr=crr: e.reciprocal(
            crr, cvr), inc="VE")
        sqr = s.add("scalar", lambda e, crr=crr, crr2=crr2: e.activation(
            crr2, crr, ACT.Sqrt), waits=[("VE", rcv)], inc="SE")
        nmv = s.add("vector", lambda e, cmu=cmu, crr2=crr2, cnm=cnm:
                    e.scalar_tensor_tensor(cnm, cmu, -1.0, crr2, AL.mult,
                                           AL.mult), waits=[("SE", sqr)],
                    inc="VE")
        lnx = s.add("scalar", lambda e, xb=xbuf, crr2=crr2, cnm=cnm,
                    out_ap=out_ap: e.activation(out_ap, xb[:, :], ACT.Identity,
                                                scale=crr2, bias=cnm),
                    waits=[("VE", nmv)], inc="SE")
        return lnx, nmv

    def _ln_pass(self, s, src_h, dst_h, pfx):
        lsA, lsB = pfx + "A", pfx + "B"
        ssA, ssB = pfx + "SA", pfx + "SB"
        garb = [self.t_junk, self.t_f_ln4]
        vep = {0: 0, 1: 0}
        sep = {0: 0, 1: 0}
        for c in range(32):
            par = c % 2
            xbuf = self.t_xa if par == 0 else self.t_xb
            lsem = lsA if par == 0 else lsB
            ld = s.dma("scalar", xbuf[:, :], dr_rows(src_h, c * 128, 128, D),
                       inc=lsem,
                       waits=[("VE", vep[par]), ("SE", sep[par])])
            vep[par] = s.add("vector", lambda e, xb=xbuf, c=c:
                             e.tensor_reduce(self.col(self.t_cola, c),
                                             xb[:, :], AX, AL.add),
                             waits=[(lsem, ld)], inc="VE")
            sep[par] = s.add("scalar", lambda e, xb=xbuf, c=c, par=par:
                             e.activation(garb[par][:, 0:1024], xb[:, :],
                                          ACT.Square,
                                          accum_out=self.col(self.t_colb, c)),
                             waits=[(lsem, ld), ("SE", sep[par])], inc="SE")
        ve0, se0 = s.val("VE"), s.val("SE")
        v1 = s.add("vector", lambda e: e.tensor_scalar(
            self.t_cola[:, 0:32], self.t_cola[:, 0:32], 1.0 / D, None,
            AL.mult), waits=[("VE", ve0), ("SE", se0)], inc="VE")
        v2 = s.add("vector", lambda e: e.tensor_scalar(
            self.t_colb[:, 0:32], self.t_colb[:, 0:32], 1.0 / D, None,
            AL.mult), waits=[("VE", v1)], inc="VE")
        v3 = s.add("vector", lambda e: e.scalar_tensor_tensor(
            self.t_colc[:, 0:32], self.t_cola[:, 0:32], -1.0,
            self.t_cola[:, 0:32], AL.mult, AL.mult), waits=[("VE", v2)],
            inc="VE")
        v4 = s.add("vector", lambda e: e.tensor_tensor(
            self.t_colc[:, 0:32], self.t_colc[:, 0:32], self.t_colb[:, 0:32],
            AL.add), waits=[("VE", v3)], inc="VE")
        v5 = s.add("vector", lambda e: e.tensor_scalar(
            self.t_colc[:, 0:32], self.t_colc[:, 0:32], EPS, None, AL.add),
            waits=[("VE", v4)], inc="VE")
        v6 = s.add("vector", lambda e: e.reciprocal(
            self.t_cold[:, 0:32], self.t_colc[:, 0:32]), waits=[("VE", v5)],
            inc="VE")
        sq = s.add("scalar", lambda e: e.activation(
            self.t_stats[:, 0:32], self.t_cold[:, 0:32], ACT.Sqrt),
            waits=[("VE", v6)], inc="SE")
        s.add("vector", lambda e: e.scalar_tensor_tensor(
            self.t_stats[:, 32:64], self.t_cola[:, 0:32], -1.0,
            self.t_stats[:, 0:32], AL.mult, AL.mult), waits=[("SE", sq)],
            inc="VE")
        s.emit()
        stp = {0: 0, 1: 0}
        lnp = {0: 0, 1: 0}
        for c in range(32):
            par = c % 2
            xbuf = self.t_xa if par == 0 else self.t_xb
            lsem = lsA if par == 0 else lsB
            ssem = ssA if par == 0 else ssB
            ld = s.dma("scalar", xbuf[:, :], dr_rows(src_h, c * 128, 128, D),
                       inc=lsem, waits=[("SE", lnp[par])])
            oap = self.t_xso3[:, par * 1024:(par + 1) * 1024]
            ln = s.add("scalar", lambda e, xb=xbuf, c=c, oap=oap:
                       e.activation(oap, xb[:, :], ACT.Identity,
                                    scale=self.col(self.t_stats, c),
                                    bias=self.col(self.t_stats, 32 + c)),
                       waits=[(lsem, ld), (ssem, stp[par])], inc="SE")
            lnp[par] = ln
            stp[par] = s.dma("gpsimd", dr_rows(dst_h, c * 128, 128, D), oap,
                             waits=[("SE", ln)], inc=ssem)
        s.add("gpsimd", lambda e: e.nop(),
              waits=[(ssA, s.val(ssA)), (ssB, s.val(ssB))])
        s.emit()

    def p3_ssm(self):
        s = self.s
        zo_ev = s.add("vector", lambda e: e.memset(self.t_cmov[0:1, :], 0.0),
                      inc="VE")
        self._zo_ev = zo_ev
        self._ln_pass(s, self.h["res"], self.h["xs3"], "LX3")

        te_x3 = 0
        lw_war = {0: 0, 1: 0}
        bank_w = {}
        ugu_war = {}
        ugs_war = 0
        gtd_ev = {}
        ub_se = {}
        for lt in range(8):
            tx = 0
            for dc in range(8):
                tx = s.dmaT("sync", self.t_x3f[:, dc * 512:(dc + 1) * 512],
                            dr_blk(self.h["xs3"], lt * 512, 512, dc * 128,
                                   128, D), inc="QX3",
                            waits=[("TE", te_x3)] if dc == 0 else [])
            for proj in ("u", "g"):
                wsrc = self.h["wuT16"] if proj == "u" else self.h["wgT16"]
                cst = self.t_cs_u if proj == "u" else self.t_cs_g
                for dc in range(8):
                    par = dc % 2
                    lsem = "LWA" if par == 0 else "LWB"
                    ldw = s.dma("scalar",
                                self.t_wst[:, par * 1024:(par + 1) * 1024],
                                dr_rows(wsrc, dc * 128, 128, D), inc=lsem,
                                waits=[("TE", lw_war[par])])
                    for fc in range(8):
                        w = []
                        if dc == 0:
                            w = bank_w.pop(fc, [])
                            if fc == 0:
                                w.append(("QX3", tx))
                        if fc == 0:
                            w.append((lsem, ldw))
                        s.add("tensor", lambda e, par=par, fc=fc, dc=dc:
                              e.matmul(
                                  self.pb[fc][0:128, 0:512],
                                  self.t_wst[:, par * 1024 + fc * 128:
                                             par * 1024 + fc * 128 + 128],
                                  self.t_x3f[:, dc * 512:(dc + 1) * 512],
                                  start=(dc == 0), stop=False), waits=w,
                              inc="TE")
                    lw_war[par] = s.val("TE")
                for fc in range(8):
                    s.add("tensor", lambda e, fc=fc, cst=cst: e.matmul(
                        self.pb[fc][0:128, 0:512],
                        cst[0:2, fc * 128:(fc + 1) * 128],
                        self.t_cmov[:, 0:512], start=False, stop=True),
                        inc="TE")
                te_p = s.val("TE")
                if proj == "u":
                    for fc in range(8):
                        gev = s.add("scalar", lambda e, fc=fc: e.activation(
                            self.t_ugu[:, fc * 512:(fc + 1) * 512],
                            self.pb[fc][0:128, 0:512], ACT.Gelu),
                            waits=[("TE", te_p)] + ugu_war.pop(fc, []),
                            inc="SE")
                        bank_w[fc] = [("SE", gev)]
                else:
                    for fc in range(8):
                        sev = s.add("scalar", lambda e, fc=fc: e.activation(
                            self.t_ugs[:, 0:512], self.pb[fc][0:128, 0:512],
                            ACT.Sigmoid),
                            waits=[("TE", te_p), ("VE", ugs_war)], inc="SE")
                        mev = s.add("vector", lambda e, fc=fc: e.tensor_tensor(
                            self.t_gtd[:, fc * 512:(fc + 1) * 512],
                            self.t_ugu[:, fc * 512:(fc + 1) * 512],
                            self.t_ugs[:, 0:512], AL.mult),
                            waits=[("SE", sev)] + gtd_ev.pop(fc, []),
                            inc="VE")
                        s.drain("vector")
                        ugs_war = mev
                        ugu_war[fc] = [("VE", mev)]
                        bank_w[fc] = [("SE", sev), ("VE", mev)]
                        gtd_ev[fc] = [("VE", mev)]
            for fc in range(8):
                par = fc % 2
                lsem = "LWA" if par == 0 else "LWB"
                ldb = s.dma("scalar",
                            self.t_wst[:, par * 1024:(par + 1) * 1024],
                            dr_rows(self.h["bmT16"], fc * 128, 128, D),
                            inc=lsem, waits=[("TE", lw_war[par])])
                for cc in range(8):
                    w = []
                    if fc == 0:
                        w = bank_w.pop(cc, []) + [("SE", ub_se.pop(cc, 0))]
                    w += gtd_ev.get(fc, []) if cc == 0 else []
                    if cc == 0:
                        w.append((lsem, ldb))
                    s.add("tensor", lambda e, par=par, fc=fc, cc=cc: e.matmul(
                        self.pb[cc][0:128, 0:512],
                        self.t_wst[:, par * 1024 + cc * 128:
                                   par * 1024 + cc * 128 + 128],
                        self.t_gtd[:, fc * 512:(fc + 1) * 512],
                        start=(fc == 0), stop=(fc == 7)), waits=w, inc="TE")
                lw_war[par] = s.val("TE")
            te_ub = s.val("TE")
            for cc in range(8):
                ub_se[cc] = s.add("scalar", lambda e, cc=cc, lt=lt: e.copy(
                    self.t_uB[:, cc * TLOC + lt * 512:cc * TLOC + lt * 512 + 512],
                    self.pb[cc][0:128, 0:512]), waits=[("TE", te_ub)],
                    inc="SE")
                bank_w[cc] = [("SE", ub_se[cc])]
            for fc in range(8):
                gtd_ev[fc] = [("TE", te_ub)]
            te_x3 = s.val("TE")
        s.emit()

        sstep = self.t_small.ap().ap[0][0]
        ubstep = self.t_uB.ap().ap[0][0]
        for cc in range(8):
            d0 = ap3(self.t_small, 8 + cc, [[sstep, 128], [0, TLOC]])
            sl = ap3(self.t_uB, cc * TLOC, [[ubstep, 128], [1, TLOC]])
            s.add("vector", lambda e, d0=d0, sl=sl: e.tensor_tensor_scan(
                sl, d0, sl, 0.0, AL.mult, AL.add), inc="VE")
            s.drain("vector")
        cex = s.add("vector", lambda e: e.tensor_copy(
            self.t_ccar[:, 0:8],
            ap3(self.t_uB, TLOC - 1, [[ubstep, 128], [TLOC, 8], [1, 1]])),
            inc="VE")
        tcar = s.add("tensor", lambda e: e.transpose(
            self.pb[4][0:8, 0:128], self.t_ccar[:, 0:8], self.t_eye[:, :]),
            waits=[("VE", cex)], inc="TE")
        vcar = s.add("vector", lambda e: e.tensor_copy(
            self.t_x4t[0:8, 0:128], self.pb[4][0:8, 0:128]),
            waits=[("TE", tcar)], inc="VE")
        stc = s.dma("gpsimd", AP(self.c["carry_in"], 0, [[128, 8], [1, 128]]),
                    self.t_x4t[0:8, 0:128], waits=[("VE", vcar)], inc="CAR")
        if self.sim1:
            for rp in range(2):
                s.add("gpsimd", lambda e, rp=rp: e.dma_start(
                    AP(self.c["carry_out"], rp * D, [[1, 1], [1, D]]),
                    AP(self.c["carry_in"], 0, [[1, 1], [1, D]])),
                    waits=[("CAR", stc)], inc="AGC", inck=16)
        else:
            s.add("gpsimd", lambda e: e.collective_compute(
                "AllGather", AL.bypass, ins=[self.c["carry_in"].ap()],
                outs=[self.c["carry_out"].ap()],
                replica_groups=[[0, 1], [2, 3], [4, 5], [6, 7]]),
                waits=[("CAR", stc)], inc="AGC")
        ldc = s.dma("scalar", self.t_x4t[0:1, 1024:2048],
                    AP(self.c["carry_out"], 0, [[1024, 1], [1, 1024]]),
                    waits=[("AGC", s.val("AGC")), ("VE", vcar)], inc="CAR2")
        for cc in range(8):
            s.add("tensor", lambda e, cc=cc: e.transpose(
                self.pb[5][0:128, cc:cc + 1],
                self.t_x4t[0:1, 1024 + cc * 128:1024 + (cc + 1) * 128],
                self.t_eye[0:1, 0:1]),
                waits=[("CAR2", ldc)] if cc == 0 else [], inc="TE")
        tca = s.val("TE")
        s.drain("vector")
        s.add("vector", lambda e: e.tensor_copy(
            self.t_ccar[:, 8:16], self.pb[5][0:128, 0:8]),
            waits=[("TE", tca)], inc="VE")
        s.drain("vector")
        msk = s.add("vector", lambda e: e.tensor_scalar(
            self.t_ccar[:, 8:16], self.t_ccar[:, 8:16],
            self.col(self.t_small, 2), None, AL.mult), inc="VE")
        s.drain("vector")
        zbc = ap3(self.t_small, 24, [[sstep, 128], [0, TLOC]])
        for cc in range(8):
            d0 = ap3(self.t_small, 8 + cc, [[sstep, 128], [0, TLOC]])
            sc = s.add("vector", lambda e, d0=d0, cc=cc: e.tensor_tensor_scan(
                self.t_ctmp[:, :], d0, zbc, self.col(self.t_ccar, 8 + cc),
                AL.mult, AL.add), waits=[("VE", msk)], inc="VE")
            s.drain("vector")
            s.add("vector", lambda e, cc=cc: e.tensor_tensor(
                self.t_uB[:, cc * TLOC:(cc + 1) * TLOC],
                self.t_uB[:, cc * TLOC:(cc + 1) * TLOC],
                self.t_ctmp[:, :], AL.add), inc="VE")
            s.drain("vector")
        s.emit()

        lc_war = {0: 0, 1: 0}
        x4ve3, str3 = 0, 0
        bank_y = {}
        for tg in range(8):
            for cc in range(8):
                par = cc % 2
                lsem = "LCA" if par == 0 else "LCB"
                ldc2 = s.dma("scalar",
                             self.t_cmsl[:, par * 1024:(par + 1) * 1024],
                             dr_rows(self.h["cmT16"], cc * 128, 128, D),
                             inc=lsem, waits=[("TE", lc_war[par])])
                for tc in range(4):
                    for dsl in range(2):
                        b = tc * 2 + dsl
                        w = []
                        if cc == 0:
                            w = bank_y.pop(b, [])
                        if tc == 0 and dsl == 0:
                            w.append((lsem, ldc2))
                        s.add("tensor", lambda e, par=par, cc=cc, tg=tg,
                              tc=tc, dsl=dsl, b=b: e.matmul(
                                  self.pb[b][0:128, 0:512],
                                  self.t_uB[0:128,
                                            cc * TLOC + tg * 512 + tc * 128:
                                            cc * TLOC + tg * 512 + tc * 128 + 128],
                                  self.t_cmsl[:, par * 1024 + dsl * 512:
                                              par * 1024 + dsl * 512 + 512],
                                  start=(cc == 0), stop=False), waits=w,
                              inc="TE")
                lc_war[par] = s.val("TE")
            for tc in range(4):
                for dsl in range(2):
                    s.add("tensor", lambda e, tc=tc, dsl=dsl: e.matmul(
                        self.pb[tc * 2 + dsl][0:128, 0:512],
                        self.t_onesr[0:1, 0:128],
                        self.t_Drow[0:1, dsl * 512:(dsl + 1) * 512],
                        start=False, stop=True), inc="TE")
            te_y = s.val("TE")
            for th in range(2):
                lxr = 0
                for i in range(2):
                    tc = th * 2 + i
                    w = [("STR3", str3), ("VE", x4ve3)] if i == 0 else []
                    if tg == 0 and th == 0 and i == 0:
                        w.append(("CAR", s.val("CAR")))
                    lxr = s.dma("sync",
                                self.t_x4t[:, i * 1024:(i + 1) * 1024],
                                dr_rows(self.h["res"], tg * 512 + tc * 128,
                                        128, D), inc="LXR3", waits=w)
                for i in range(2):
                    tc = th * 2 + i
                    ev = 0
                    for dsl in range(2):
                        b = tc * 2 + dsl
                        w = [("TE", te_y), ("LXR3", lxr)] if (i == 0 and
                                                              dsl == 0) else []
                        ev = s.add("vector", lambda e, b=b, i=i, dsl=dsl:
                                   e.tensor_tensor(
                                       self.t_x4t[:, i * 1024 + dsl * 512:
                                                  i * 1024 + dsl * 512 + 512],
                                       self.t_x4t[:, i * 1024 + dsl * 512:
                                                  i * 1024 + dsl * 512 + 512],
                                       self.pb[b][0:128, 0:512], AL.add),
                                   waits=w, inc="VE")
                        bank_y[b] = [("VE", ev)]
                    s.dma("gpsimd",
                          dr_rows(self.h["res"], tg * 512 + tc * 128, 128, D),
                          self.t_x4t[:, i * 1024:(i + 1) * 1024],
                          waits=[("VE", ev)], inc="STR3")
                x4ve3 = s.val("VE")
                str3 = s.val("STR3")
        s.add("gpsimd", lambda e: e.nop(), waits=[("STR3", s.val("STR3"))])
        if "S0" in self.o_dbg:
            ve = s.val("VE")
            s.dma("gpsimd", self.o_dbg["S0"].ap(), self.t_uB[:, :],
                  waits=[("VE", ve)], inc="ST")
        if "x3" in self.o_dbg:
            s.dma("gpsimd", self.o_dbg["x3"].ap(), self.h["res"].ap(),
                  inc="ST")
        s.add("gpsimd", lambda e: e.nop(), waits=[("ST", s.val("ST"))])
        s.emit()

    def p4_mlp(self):
        s = self.s
        self._ln_pass(s, self.h["res"], self.h["xs4"], "LX4")

        te_x4 = 0
        sev_s = 0
        for sg in range(8):
            tx = 0
            for dc in range(8):
                tx = s.dmaT("sync", self.t_x4f[:, dc * 512:(dc + 1) * 512],
                            dr_blk(self.h["xs4"], sg * 512, 512, dc * 128,
                                   128, D), inc="QX4",
                            waits=[("TE", te_x4)] if dc == 0 else [])
            bank = self.pb[sg]
            for dc in range(8):
                w = [("QX4", tx), ("SE", sev_s)] if dc == 0 else []
                s.add("tensor", lambda e, dc=dc, bank=bank: e.matmul(
                    bank[0:64, 0:512],
                    self.t_win16[:, dc * 64:(dc + 1) * 64],
                    self.t_x4f[:, dc * 512:(dc + 1) * 512],
                    start=(dc == 0), stop=False), waits=w, inc="TE")
            s.add("tensor", lambda e, bank=bank: e.matmul(
                bank[0:64, 0:512], self.t_cs_in[0:2, 0:64],
                self.t_cmov[:, 0:512], start=False, stop=True), inc="TE")
            tev = s.val("TE")
            sev_s = s.add("scalar", lambda e, sg=sg, bank=bank: e.copy(
                self.t_s4[:, sg * 512:(sg + 1) * 512], bank[0:64, 0:512]),
                waits=[("TE", tev)], inc="SE")
            te_x4 = s.val("TE")
        s.emit()

        vprev = 0
        for i in range(16):
            hbuf = self.t_hia if i % 2 == 0 else self.t_hib
            se = s.add("scalar", lambda e, i=i, hbuf=hbuf: e.activation(
                hbuf[:, :], self.t_s4[:, :], ACT.Relu,
                scale=self.col(self.t_w1s, i), bias=self.col(self.t_b1s, i)),
                waits=[("SE", sev_s), ("VE", vprev)] if i < 2 else
                [("VE", vprev)], inc="SE")
            if i == 0:
                b2bc = ap3(self.t_b2s, 0,
                           [[self.t_b2s.ap().ap[0][0], 64], [0, TLOC]])
                vprev = s.add("vector", lambda e, hbuf=hbuf, b2bc=b2bc:
                              e.scalar_tensor_tensor(
                                  self.t_g4[:, :], hbuf[:, :],
                                  self.col(self.t_w2s, 0), b2bc, AL.mult,
                                  AL.add), waits=[("SE", se)], inc="VE")
            else:
                vprev = s.add("vector", lambda e, i=i, hbuf=hbuf:
                              e.scalar_tensor_tensor(
                                  self.t_g4[:, :], hbuf[:, :],
                                  self.col(self.t_w2s, i), self.t_g4[:, :],
                                  AL.mult, AL.add), waits=[("SE", se)],
                              inc="VE")
            s.drain("vector")
        gcast = s.add("vector", lambda e: e.tensor_copy(
            self.t_g4b[:, :], self.t_g4[:, :]), inc="VE")
        s.emit()

        x4ve4, str4 = 0, 0
        bank_o = {}
        for tg in range(8):
            for tc in range(4):
                for dsl in range(2):
                    b = tc * 2 + dsl
                    w = bank_o.pop(b, [])
                    s.add("tensor", lambda e, tg=tg, tc=tc, dsl=dsl, b=b:
                          e.matmul(
                              self.pb[b][0:128, 0:512],
                              self.t_g4b[0:64, tg * 512 + tc * 128:
                                         tg * 512 + tc * 128 + 128],
                              self.t_wout16[0:64, dsl * 512:(dsl + 1) * 512],
                              start=True, stop=False), waits=w, inc="TE")
                    s.add("tensor", lambda e, tc=tc, dsl=dsl, b=b: e.matmul(
                        self.pb[b][0:128, 0:512], self.t_onesr[0:1, 0:128],
                        self.t_borow[0:1, dsl * 512:(dsl + 1) * 512],
                        start=False, stop=True), inc="TE")
            te_o = s.val("TE")
            for th in range(2):
                lxr = 0
                for i in range(2):
                    tc = th * 2 + i
                    w = [("STO", str4), ("VE", x4ve4)] if i == 0 else []
                    lxr = s.dma("sync",
                                self.t_x4t[:, i * 1024:(i + 1) * 1024],
                                dr_rows(self.h["res"], tg * 512 + tc * 128,
                                        128, D), inc="LXR4", waits=w)
                for i in range(2):
                    tc = th * 2 + i
                    ev = 0
                    for dsl in range(2):
                        b = tc * 2 + dsl
                        w = [("TE", te_o), ("LXR4", lxr)] if (i == 0 and
                                                              dsl == 0) else []
                        ev = s.add("vector", lambda e, b=b, i=i, dsl=dsl:
                                   e.tensor_tensor(
                                       self.t_x4t[:, i * 1024 + dsl * 512:
                                                  i * 1024 + dsl * 512 + 512],
                                       self.t_x4t[:, i * 1024 + dsl * 512:
                                                  i * 1024 + dsl * 512 + 512],
                                       self.pb[b][0:128, 0:512], AL.add),
                                   waits=w, inc="VE")
                        bank_o[b] = [("VE", ev)]
                    s.dma("gpsimd",
                          dr_rows(self.o_out, tg * 512 + tc * 128, 128, D),
                          self.t_x4t[:, i * 1024:(i + 1) * 1024],
                          waits=[("VE", ev)], inc="STO")
                x4ve4 = s.val("VE")
                str4 = s.val("STO")
        s.add("gpsimd", lambda e: e.nop(), waits=[("STO", s.val("STO"))])
        s.emit()

    def finish(self):
        s = self.s
        st = s.val("ST")
        if st:
            s.add("gpsimd", lambda e: e.nop(), waits=[("ST", st)])
        s.emit()


def _rot(a, sigma):
    if sigma == 0:
        return np.ascontiguousarray(a)
    return np.ascontiguousarray(np.concatenate([a[TLOC:], a[:TLOC]], axis=0))


def kernel(**inputs):
    f32 = np.float32
    inp = inputs
    common = {}
    common["wqT"] = np.ascontiguousarray(np.asarray(inp["L0_attn_Wq"]).T, f32)
    common["wkT"] = np.ascontiguousarray(np.asarray(inp["L0_attn_Wk"]).T, f32)
    common["wvT"] = np.ascontiguousarray(np.asarray(inp["L0_attn_Wv"]).T, f32)
    common["bq"] = np.asarray(inp["L0_attn_bq"], f32).reshape(1, D)
    common["bk"] = np.asarray(inp["L0_attn_bk"], f32).reshape(1, D)
    common["bv"] = np.asarray(inp["L0_attn_bv"], f32).reshape(1, D)
    common["wuT"] = np.ascontiguousarray(np.asarray(inp["L1_attn_W_u"]).T, f32)
    common["wgT"] = np.ascontiguousarray(np.asarray(inp["L1_attn_W_g"]).T, f32)
    common["bu"] = np.asarray(inp["L1_attn_b_u"], f32).reshape(1, D)
    common["bg"] = np.asarray(inp["L1_attn_b_g"], f32).reshape(1, D)
    common["Am"] = np.asarray(inp["L1_attn_A"], f32)
    common["bmT"] = np.ascontiguousarray(np.asarray(inp["L1_attn_Bm"]).T, f32)
    common["cmT"] = np.ascontiguousarray(np.asarray(inp["L1_attn_Cm"]).T, f32)
    common["Dv"] = np.asarray(inp["L1_attn_D"], f32).reshape(1, D)
    common["win"] = np.asarray(inp["L1_ffn_Win"], f32)
    common["binf"] = np.asarray(inp["L1_ffn_bin_"], f32).reshape(1, 64)
    common["w1"] = np.asarray(inp["L1_ffn_W1"], f32)
    common["b1"] = np.asarray(inp["L1_ffn_b1"], f32)
    common["w2"] = np.asarray(inp["L1_ffn_W2"], f32)
    common["b2"] = np.asarray(inp["L1_ffn_b2"], f32).reshape(1, 64)
    common["wout"] = np.asarray(inp["L1_ffn_Wout"], f32)
    common["bout"] = np.asarray(inp["L1_ffn_bout"], f32).reshape(1, D)
    wg = np.asarray(inp["L0_ffn_w_gate"], f32)
    wu = np.asarray(inp["L0_ffn_w_up"], f32)
    wd = np.asarray(inp["L0_ffn_w_down"], f32)
    x = np.asarray(inp["L0_attn_x"], f32)
    maps = []
    for c in range(8):
        b, sg = c // 2, c % 2
        m = dict(common)
        m["x"] = _rot(x[b], sg)
        m["wg_sh"] = np.ascontiguousarray(wg[c * WSH_GU:(c + 1) * WSH_GU])
        m["wu_sh"] = np.ascontiguousarray(wu[c * WSH_GU:(c + 1) * WSH_GU])
        m["wd_sh"] = np.ascontiguousarray(wd[:, c * WSH_GU:(c + 1) * WSH_GU])
        cm = np.zeros((1, 8), f32)
        cm[0, 0] = 1.0 if sg == 1 else 0.0
        m["cmask"] = cm
        maps.append(m)
    p = Prog()
    res = run_bass_kernel_spmd(p.nc, maps, core_ids=list(range(8)))
    out = np.empty((B, L, D), np.float32)
    for c in range(8):
        b, sg = c // 2, c % 2
        o = np.asarray(res.results[c]["out"], np.float32)
        if sg == 0:
            out[b, 0:TLOC] = o
        else:
            out[b, TLOC:L] = o
    return out
